# version 01
# baseline (speedup 1.0000x reference)
import numpy as np
import ml_dtypes
from contextlib import ExitStack
import concourse.bass as bass
import concourse.mybir as mybir

F32 = mybir.dt.float32
BF16 = mybir.dt.bfloat16
AF = mybir.ActivationFunctionType
ALU = mybir.AluOpType
AX = mybir.AxisListType
BF = ml_dtypes.bfloat16

NCORE = 8
T = 1024
TP = 1152
O0 = 128
D = 4096
DK = 32
DSSM = 8192
NH = 64
EPS = 1e-5
INV_SQRT_HD = float(1.0 / np.sqrt(128.0))
EXPC = 10.0


def _blk(W, c0, c1):
    w = W[:, c0:c1]
    return np.ascontiguousarray(
        w.reshape(32, 128, c1 - c0).transpose(1, 0, 2).reshape(128, 32 * (c1 - c0)))


def host_prep(inputs):
    inp = {k: np.ascontiguousarray(np.asarray(v, dtype=np.float32))
           for k, v in inputs.items()}
    h = {}
    W_in = inp["L0_attn_W_in"]
    blocks = [_blk(W_in, b * 128, (b + 1) * 128) for b in range(64)]
    blocks += [_blk(W_in, 8192 + b * 128, 8192 + (b + 1) * 128) for b in range(64)]
    blocks.append(_blk(W_in, 16384, 16512))
    blocks.append(_blk(W_in, 16512, 16640))
    dtb = _blk(W_in, 16640, 16704)
    blocks.append(np.concatenate([dtb, np.zeros_like(dtb)], axis=1))
    h["W_in_r"] = np.ascontiguousarray(np.stack(blocks))

    W_out = inp["L0_attn_W_out"]
    h["W_out_r"] = np.ascontiguousarray(np.stack(
        [W_out[:, b * 128:(b + 1) * 128].reshape(64, 128, 128).transpose(1, 0, 2)
         .reshape(128, 8192) for b in range(32)]))

    cw = inp["L0_attn_conv_w"]
    h["convw_r"] = np.ascontiguousarray(
        cw.reshape(66, 128, 4).transpose(1, 0, 2).reshape(128, 264))

    h["dtb"] = inp["L0_attn_dt_bias"].reshape(64, 1)
    h["alog"] = inp["L0_attn_A_log"].reshape(64, 1)
    h["dpar"] = inp["L0_attn_D_param"].reshape(64, 1)
    h["nw_r"] = np.ascontiguousarray(inp["L0_attn_norm_weight"].reshape(64, 128).T)

    h["routeT"] = _blk(np.ascontiguousarray(inp["L0_ffn_route_w"].T), 0, 8)
    h["A0_r"] = _blk(np.ascontiguousarray(
        inp["L0_ffn_A"].transpose(2, 0, 1).reshape(4096, 128)), 0, 128)
    h["Bw0_r"] = np.ascontiguousarray(
        inp["L0_ffn_Bw"].transpose(0, 2, 1).reshape(128, 4096))

    wq = inp["L1_attn_wq"]
    wql = []
    for kv in range(8):
        gs = [_blk(wq, (4 * kv + g) * 128, (4 * kv + g + 1) * 128) for g in range(4)]
        wql.append(np.concatenate(gs, axis=1))
    h["wq_r"] = np.ascontiguousarray(np.stack(wql))
    h["wk_r"] = np.ascontiguousarray(np.stack(
        [_blk(inp["L1_attn_wk"], kv * 128, (kv + 1) * 128) for kv in range(8)]))
    h["wv_r"] = np.ascontiguousarray(np.stack(
        [_blk(inp["L1_attn_wv"], kv * 128, (kv + 1) * 128) for kv in range(8)]))
    wo = inp["L1_attn_wo"]
    h["wo_r"] = np.ascontiguousarray(np.stack(
        [wo[:, b * 128:(b + 1) * 128].reshape(8, 128, 128).transpose(1, 0, 2)
         .reshape(128, 1024) for b in range(32)]))

    WbT = np.ascontiguousarray(inp["L1_ffn_W_base"].T)
    h["Wb_r"] = np.ascontiguousarray(np.stack(
        [_blk(WbT, b * 128, (b + 1) * 128) for b in range(32)]))
    h["bb_r"] = np.ascontiguousarray(inp["L1_ffn_b_base"].reshape(32, 128).T)
    h["wrt_r"] = _blk(np.ascontiguousarray(inp["L1_ffn_W_router"].T), 0, 8)
    h["brt"] = inp["L1_ffn_b_router"].reshape(8, 1)
    h["lA1_r"] = _blk(np.ascontiguousarray(
        inp["L1_ffn_lora_A"].transpose(1, 0, 2).reshape(4096, 128)), 0, 128)
    h["lB1_r"] = np.ascontiguousarray(inp["L1_ffn_lora_B"].reshape(128, 4096))

    x = inp["L0_attn_hidden_states"].reshape(8192, 4096)
    xs, rms = [], []
    for i in range(NCORE):
        sl = np.zeros((TP, D), np.float32)
        sl[O0:O0 + T] = x[i * T:(i + 1) * T]
        if i > 0:
            sl[O0 - 3:O0] = x[i * T - 3:i * T]
        xs.append(np.ascontiguousarray(sl))
        rm = np.zeros((8, 1), np.float32)
        rm[:i] = 1.0
        rms.append(np.ascontiguousarray(rm))
    return h, xs, rms


class Sem:
    __slots__ = ("h", "n")
    def __init__(self, h):
        self.h = h
        self.n = 0


class Emit:
    KEYS = ("t", "v", "s", "g", "y")

    def __init__(self, nc, ctx):
        self.nc = nc
        self.ctx = ctx
        self.q = {k: [] for k in self.KEYS}

    def sem(self, name):
        return Sem(self.ctx.enter_context(self.nc.semaphore(name)))

    def op(self, ek, fn, inc=None, k=1):
        if inc is None:
            self.q[ek].append(fn)
            return None
        inc.n += k
        n, h = inc.n, inc.h
        self.q[ek].append(lambda e, fn=fn, h=h, k=k: fn(e).then_inc(h, k))
        return n

    def dma(self, ek, out, in_, inc, **kw):
        if inc.n > 0:
            self.wait(ek, inc, inc.n)
        return self.op(ek, lambda e, o=out, i=in_, kw=dict(kw): e.dma_start(out=o, in_=i, **kw),
                       inc, k=16)

    def wait(self, ek, sem, n=None):
        if n is None:
            n = sem.n
        if n > 0:
            self.q[ek].append(lambda e, h=sem.h, n=n: e.wait_ge(h, n))

    def install(self, block):
        def make(ops):
            def run(eng):
                for f in ops:
                    f(eng)
            return run
        for key, attr in (("t", "tensor"), ("v", "vector"), ("s", "scalar"),
                          ("g", "gpsimd"), ("y", "sync")):
            if self.q[key]:
                getattr(block, attr)(make(self.q[key]))


SCALAR_CONSTS = [EPS, 1.0 / D, 1.0 / DSSM, -EXPC, INV_SQRT_HD, 1.0 / 64.0, -1e30]


def make_scalar_consts(nc, ctx):
    vals = SCALAR_CONSTS
    arr = np.tile(np.asarray(vals, np.float32)[None, :], (128, 1))
    dram = nc.inline_tensor(np.ascontiguousarray(arr), name="cSCAL")
    tile = ctx.enter_context(nc.sbuf_tensor("sCSC", [128, len(vals)], mybir.dt.float32))
    for i, v in enumerate(vals):
        nc.const_aps.register(mybir.dt.float32, float(v), tile[:, i:i + 1])
    return tile, dram


def make_consts(nc):
    c = {}
    c["IDF32"] = nc.inline_tensor(np.eye(128, dtype=np.float32), name="cIDF32")
    c["IDBF"] = nc.inline_tensor(np.eye(128, dtype=BF), name="cIDBF")
    tril = np.where(np.arange(128)[:, None] <= np.arange(128)[None, :], 0.0, -1e30)
    c["MASKC"] = nc.inline_tensor(
        np.ascontiguousarray(np.tile(tril, (1, 8))).astype(np.float32), name="cMASKC")
    sel = np.zeros((8, 1024), np.float32)
    for cc in range(8):
        sel[cc, cc * 128:(cc + 1) * 128] = 1.0
    c["SELC"] = nc.inline_tensor(sel, name="cSELC")
    c["ONES1F"] = nc.inline_tensor(np.ones((1, 128), np.float32), name="cONES1F")
    c["ONESCOLB"] = nc.inline_tensor(np.ones((128, 1), BF), name="cONESCOLB")
    ex = np.zeros((8, 128), np.float32)
    for e8 in range(8):
        ex[e8, e8 * 16:(e8 + 1) * 16] = 1.0
    c["EXPAND16"] = nc.inline_tensor(ex, name="cEXPAND16")
    c["IDENT8"] = nc.inline_tensor(
        np.ascontiguousarray(np.tile(np.eye(128, dtype=BF), (1, 8))), name="cIDENT8")
    c["ZERO64"] = nc.inline_tensor(np.zeros((64, 128), np.float32), name="cZERO64")
    return c


def build_ln(E, tl, sm, x_dram, slab, ntile, slab_w, dst_off):
    s_pe, s_dv, s_ac, s_yq = sm["pe"], sm["dv"], sm["ac"], sm["yq"]
    xa = tl["ln_x"]
    xc = tl["ln_xc"]
    sqf = tl["ln_sqf"]
    xnb = tl["ln_xn"]
    col = tl["ln_col"]
    ptr = tl["ps_trb"]
    idbf = tl["IDBF_sb"]
    ld_done = [0, 0]
    prev_tr = 0
    prev_sq = 0

    for it in range(ntile):
        b = it % 2
        if it >= 2:
            E.wait("y", s_dv, ld_done[b])
        n_ld = E.dma("y", xa[b][:, :], x_dram[it * 128:(it + 1) * 128, :], s_yq)
        E.wait("v", s_yq, n_ld)
        E.op("v", lambda e, b=b: e.tensor_reduce(col[:, 0:1], xa[b][:, :], AX.X, ALU.add), s_dv)
        E.wait("s", s_dv)
        E.op("s", lambda e: e.mul(col[:, 1:2], col[:, 0:1], float(1.0 / D)), s_ac)
        E.wait("v", s_ac)
        E.wait("v", s_ac, prev_sq)
        E.op("v", lambda e, b=b: e.tensor_scalar(
            xc[:, :], xa[b][:, :], col[:, 1:2], None, ALU.subtract), s_dv)
        ld_done[b] = s_dv.n
        E.wait("s", s_dv)
        E.op("s", lambda e: e.activation(
            sqf[:, :], xc[:, :], AF.Square, accum_out=col[:, 2:3]), s_ac)
        prev_sq = s_ac.n
        E.wait("s", s_ac)
        E.op("s", lambda e: e.activation(
            col[:, 3:4], col[:, 2:3], AF.Sqrt, bias=float(EPS), scale=float(1.0 / D)), s_ac)
        E.wait("v", s_ac)
        E.op("v", lambda e: e.reciprocal(col[:, 4:5], col[:, 3:4]), s_dv)
        E.wait("v", s_dv)
        E.wait("v", s_pe, prev_tr)
        E.op("v", lambda e: e.tensor_scalar(
            xnb[:, :], xc[:, :], col[:, 4:5], None, ALU.mult), s_dv)
        n_xn = s_dv.n
        for gidx in range(4):
            E.wait("t", s_dv, n_xn)
            E.wait("t", s_ac)
            for j in range(8):
                ci = gidx * 8 + j
                E.op("t", lambda e, ci=ci, j=j: e.transpose(
                    ptr[:, j * 128:(j + 1) * 128], xnb[:, ci * 128:(ci + 1) * 128],
                    idbf[:, :]), s_pe)
            E.wait("s", s_pe)
            for j in range(8):
                ci = gidx * 8 + j
                dst0 = ci * slab_w + dst_off + it * 128
                E.op("s", lambda e, d0=dst0, j=j: e.copy(
                    slab[:, d0:d0 + 128], ptr[:, j * 128:(j + 1) * 128]), s_ac)
        prev_tr = s_pe.n


def conv_silu(E, sm, xraw_bf, cacc, out_bf, cwcol_tile, ch_tile_idx):
    s_dv, s_ac = sm["dv"], sm["ac"]
    c0 = ch_tile_idx * 4
    E.op("v", lambda e: e.tensor_scalar(
        cacc[0][:, :], xraw_bf[:, 0:1024], cwcol_tile[:, c0:c0 + 1], None, ALU.mult), s_dv)
    for k in range(1, 4):
        src, dst = cacc[(k + 1) % 2], cacc[k % 2]
        E.op("v", lambda e, k=k, src=src, dst=dst: e.scalar_tensor_tensor(
            dst[:, :], xraw_bf[:, k:1024 + k], cwcol_tile[:, c0 + k:c0 + k + 1],
            src[:, :], ALU.mult, ALU.add), s_dv)
    n = s_dv.n
    E.wait("s", s_dv, n)
    E.op("s", lambda e: e.activation(out_bf[:, :], cacc[1][:, :], AF.Silu), s_ac)
    return s_ac.n, n


def proj_block(E, sm, wslab_dram, wbuf, ps, pbase, XN, col_lo, col_hi,
               tile_w=128, wbuf_free_pe=0, w_preloaded=None, ps_free=None,
               slab_w=TP):
    s_pe, s_gq = sm["pe"], sm["gq"]
    if w_preloaded is None:
        E.wait("g", s_pe, wbuf_free_pe)
        n_w = E.dma("g", wbuf[:, 0:32 * tile_w], wslab_dram, s_gq)
    else:
        n_w = w_preloaded
    E.wait("t", s_gq, n_w)
    if ps_free is not None:
        E.wait("t", ps_free[0], ps_free[1])
    width = col_hi - col_lo
    nmm = (width + 511) // 512
    for k in range(32):
        for m in range(nmm):
            a = m * 512
            b = min(width, a + 512)
            E.op("t", lambda e, k=k, a=a, b=b: e.matmul(
                ps[0:tile_w, pbase + a:pbase + b],
                wbuf[:, k * tile_w:(k + 1) * tile_w],
                XN[:, k * slab_w + col_lo + a:k * slab_w + col_lo + b],
                start=(k == 0), stop=(k == 31)), s_pe)
    return s_pe.n, n_w


def build_mamba_preamble(E, tl, sm, dr):
    s_pe, s_dv, s_ac, s_po = sm["pe"], sm["dv"], sm["ac"], sm["po"]
    s_gq, s_sq, s_yq = sm["gq"], sm["sq"], sm["yq"]
    XN = tl["XN"][:, :]
    psp = tl["ps_proj"]
    trb = tl["ps_trb"]
    trf = tl["ps3"]

    for nm, t in (("dtb", tl["dtb"]), ("alog", tl["alog"]), ("dpar", tl["dpar"]),
                  ("convw", tl["convw"])):
        n_sm = E.dma("y", t[:, :], dr[nm], s_yq)
    for ek in ("v", "s", "t"):
        E.wait(ek, s_yq, n_sm)

    n_cs = 0
    wxf = [0, 0]
    for i, out in ((0, "Bc"), (1, "Cc")):
        npe, _ = proj_block(E, sm, dr["W_in_r"][128 + i], tl["wx"][i % 2], psp, 0,
                            XN, O0 - 3, TP)
        wxf[i % 2] = npe
        E.wait("s", s_pe, npe)
        E.op("s", lambda e: e.copy(tl["xraw_bf"][:, 0:1027], psp[0:128, 0:1027]), s_ac)
        n_ev = s_ac.n
        E.wait("v", s_ac, n_ev)
        n_cs, n_convdv = conv_silu(E, sm, tl["xraw_bf"], tl["cacc"], tl[out],
                                   tl["convw"], 64 + i)
        E.wait("t", s_ac, n_ev)
    E.wait("s", s_ac, n_cs)
    E.op("s", lambda e: e.copy(tl["CcF"][:, :], tl["Cc"][:, :]), s_ac)

    E.wait("t", s_ac)
    for c in range(8):
        E.op("t", lambda e, c=c: e.transpose(
            trb[:, c * 128:(c + 1) * 128], tl["Bc"][:, c * 128:(c + 1) * 128],
            tl["IDBF_sb"][:, :]), s_pe)
    E.wait("s", s_pe)
    E.op("s", lambda e: e.copy(tl["B_tok"][:, :], trb[:, :]), s_ac)
    n_btok = s_ac.n

    for c in range(8):
        E.op("t", lambda e, c=c: e.matmul(
            psp[0:128, c * 128:(c + 1) * 128], tl["Bc"][:, c * 128:(c + 1) * 128],
            tl["Cc"][:, c * 128:(c + 1) * 128], start=True, stop=True), s_pe)
    n_g = s_pe.n
    E.wait("v", s_pe, n_g)
    E.op("v", lambda e: e.tensor_copy(tl["G_all"][:, 0:512], psp[0:128, 0:512]), s_dv)
    E.op("v", lambda e: e.tensor_copy(tl["G_all"][:, 512:1024], psp[0:128, 512:1024]), s_dv)
    n_gall = s_dv.n

    npe, _ = proj_block(E, sm, dr["W_in_r"][130][:, 0:2048], tl["wx"][0], psp, 0,
                        XN, O0, TP, tile_w=64, wbuf_free_pe=n_g,
                        ps_free=(s_dv, n_gall))
    wxf[0] = npe
    E.wait("s", s_pe, npe)
    E.op("s", lambda e: e.copy(tl["dtraw"][:, :], psp[0:64, 0:1024]), s_ac)
    n_dtr = s_ac.n
    E.wait("s", s_ac, n_dtr)
    E.op("s", lambda e: e.activation(
        tl["spa"][:, :], tl["dtraw"][:, :], AF.Abs, bias=tl["dtb"][:, 0:1]), s_ac)
    E.op("s", lambda e: e.activation(tl["spb"][:, :], tl["spa"][:, :], AF.Exp,
                                     scale=-1.0), s_ac)
    E.op("s", lambda e: e.activation(tl["spa"][:, :], tl["spb"][:, :], AF.Ln,
                                     bias=1.0), s_ac)
    E.op("s", lambda e: e.activation(tl["spb"][:, :], tl["dtraw"][:, :], AF.Relu,
                                     bias=tl["dtb"][:, 0:1]), s_ac)
    n_sp = s_ac.n
    E.wait("v", s_ac, n_sp)
    E.op("v", lambda e: e.tensor_add(tl["dtsp"][:, :], tl["spa"][:, :],
                                     tl["spb"][:, :]), s_dv)
    n_dtsp = s_dv.n
    E.op("s", lambda e: e.activation(tl["dcol"][:, 0:1], tl["alog"][:, 0:1], AF.Exp), s_ac)
    n_na = s_ac.n
    E.wait("s", s_ac, n_na)
    E.op("s", lambda e: e.mul(tl["dcol"][:, 1:2], tl["dcol"][:, 0:1], -1.0), s_ac)
    n_na2 = s_ac.n
    E.wait("v", s_ac, n_na2)
    E.wait("v", s_dv, n_dtsp)
    E.op("v", lambda e: e.tensor_scalar(
        tl["logdA"][:, :], tl["dtsp"][:, :], tl["dcol"][:, 1:2], None, ALU.mult), s_dv)
    n_ld = s_dv.n
    E.wait("s", s_dv, n_dtsp)
    E.op("s", lambda e: e.activation(tl["spb"][:, :], tl["dtsp"][:, :], AF.Ln), s_ac)
    n_lndt = s_ac.n
    E.wait("v", s_dv, n_ld)
    for c in range(8):
        E.op("v", lambda e, c=c: e.tensor_tensor_scan(
            tl["cums"][:, c * 128:(c + 1) * 128], tl["logdA"][:, c * 128:(c + 1) * 128],
            tl["ZERO64_sb"][:, :], 0.0, ALU.add, ALU.add), s_dv)
    n_cums = s_dv.n
    E.wait("v", s_ac, n_lndt)
    E.op("v", lambda e: e.tensor_sub(tl["cmlneg"][:, :], tl["spb"][:, :],
                                     tl["cums"][:, :]), s_dv)
    n_cml = s_dv.n
    E.op("v", lambda e: e.memset(tl["ptot"][:, 0:1], 0.0), s_dv)
    for c in range(1, 8):
        E.wait("v", s_dv)
        E.op("v", lambda e, c=c: e.tensor_add(
            tl["ptot"][:, c:c + 1], tl["ptot"][:, c - 1:c],
            tl["cums"][:, c * 128 - 1:c * 128]), s_dv)
    E.wait("v", s_dv)
    E.op("v", lambda e: e.tensor_add(
        tl["coretot"][:, 0:1], tl["ptot"][:, 7:8], tl["cums"][:, 1023:1024]), s_dv)
    n_ptot = s_dv.n
    E.wait("v", s_dv, n_ptot)
    for c in range(8):
        E.op("v", lambda e, c=c: e.tensor_scalar(
            tl["spa"][:, c * 128:(c + 1) * 128], tl["cums"][:, c * 128:(c + 1) * 128],
            tl["ptot"][:, c:c + 1], None, ALU.add), s_dv)
    n_corec = s_dv.n
    E.wait("s", s_dv, n_corec)
    E.op("s", lambda e: e.activation(tl["ecc"][:, :], tl["spa"][:, :], AF.Exp), s_ac)
    n_ecc = s_ac.n
    E.wait("v", s_ac, n_ecc)
    for c in range(8):
        E.wait("v", s_dv)
        E.op("v", lambda e, c=c: e.tensor_scalar(
            tl["spa"][:, c * 128:(c + 1) * 128], tl["cums"][:, c * 128:(c + 1) * 128],
            tl["cums"][:, c * 128 + 127:c * 128 + 128], None, ALU.subtract), s_dv)
    n_warg = s_dv.n
    E.wait("s", s_dv, n_warg)
    E.op("s", lambda e: e.activation(tl["spb"][:, :], tl["spa"][:, :], AF.Exp,
                                     scale=-1.0), s_ac)
    n_wexp = s_ac.n
    E.wait("v", s_ac, n_wexp)
    E.op("v", lambda e: e.tensor_mul(tl["wdtT"][:, :], tl["spb"][:, :],
                                     tl["dtsp"][:, :]), s_dv)
    n_wdtT = s_dv.n

    E.wait("t", s_dv, n_wdtT)
    for c in range(8):
        E.op("t", lambda e, c=c: e.transpose(
            trf[:, c * 64:(c + 1) * 64], tl["cums"][0:64, c * 128:(c + 1) * 128],
            tl["IDF32_sb"][0:64, 0:64]), s_pe)
    E.wait("s", s_pe)
    E.op("s", lambda e: e.copy(tl["cumstok"][:, :], trf[:, 0:512]), s_ac)
    n_ct = s_ac.n
    E.wait("s", s_ac, n_ct)
    E.op("s", lambda e: e.activation(tl["expcumstok"][:, :], tl["cumstok"][:, :],
                                     AF.Exp), s_ac)
    E.wait("t", s_ac, n_ct)
    for c in range(8):
        E.op("t", lambda e, c=c: e.transpose(
            trf[:, c * 64:(c + 1) * 64], tl["wdtT"][0:64, c * 128:(c + 1) * 128],
            tl["IDF32_sb"][0:64, 0:64]), s_pe)
    E.wait("s", s_pe)
    E.op("s", lambda e: e.copy(tl["wdt_tok"][:, :], trf[:, 0:512]), s_ac)
    n_wt = s_ac.n

    E.wait("t", s_ac, n_wt)
    E.op("t", lambda e: e.transpose(trf[0:1, 0:64], tl["dpar"][0:64, 0:1],
                                    tl["IDF32_sb"][0:64, 0:64]), s_pe)
    n_dtr2 = s_pe.n
    E.wait("s", s_pe, n_dtr2)
    E.op("s", lambda e: e.copy(tl["rowf"][0:1, 0:64], trf[0:1, 0:64]), s_ac)
    n_drow = s_ac.n
    E.wait("t", s_ac, n_drow)
    E.op("t", lambda e: e.matmul(trf[0:128, 0:64], tl["ONES1F_sb"][0:1, :],
                                 tl["rowf"][0:1, 0:64], start=True, stop=True), s_pe)
    n_dbc = s_pe.n
    E.wait("v", s_pe, n_dbc)
    E.op("v", lambda e: e.tensor_copy(tl["Dbc"][:, :], trf[0:128, 0:64]), s_dv)
    n_dbcv = s_dv.n

    for c in range(8):
        E.op("v", lambda e, c=c: e.tensor_copy(
            tl["totm"][:, c:c + 1], tl["cums"][:, c * 128 + 127:c * 128 + 128]), s_dv)
    n_totm = s_dv.n
    E.wait("t", s_dv, n_totm)
    E.wait("t", s_dv, n_dbcv)
    E.op("t", lambda e: e.transpose(trf[0:8, 0:64], tl["totm"][0:64, 0:8],
                                    tl["IDF32_sb"][0:64, 0:64]), s_pe)
    n_ttr = s_pe.n
    E.wait("s", s_pe, n_ttr)
    E.op("s", lambda e: e.copy(tl["totT"][0:8, 0:64], trf[0:8, 0:64]), s_ac)
    n_totT = s_ac.n
    E.wait("y", s_ac, n_totT)
    n_fl = E.dma("y", tl["rowf2"][0:1, 0:512], tl["totT"][0:8, 0:64], s_yq)
    E.wait("t", s_yq, n_fl)
    E.op("t", lambda e: e.matmul(trf[0:128, 0:512], tl["ONES1F_sb"][0:1, :],
                                 tl["rowf2"][0:1, 0:512], start=True, stop=True), s_pe)
    n_tb = s_pe.n
    E.wait("s", s_pe, n_tb)
    E.op("s", lambda e: e.activation(tl["DtotB"][:, :], trf[0:128, 0:512], AF.Exp), s_ac)
    n_dtot = s_ac.n
    return dict(n_gall=n_gall, n_ecc=n_ecc, n_dtot=n_dtot, n_wt=n_wt,
                n_cml=n_cml, n_cums=n_cums, wx0_free=wxf[0], wx1_free=wxf[1],
                n_ac_final=s_ac.n, n_dv_final=s_dv.n, n_pe_final=s_pe.n)


def build_mamba_phase_a(E, tl, sm, dr, pre, nheads=64):
    s_pe, s_dv, s_ac, s_po = sm["pe"], sm["dv"], sm["ac"], sm["po"]
    s_gq, s_sq, s_yq, s_st = sm["gq"], sm["sq"], sm["yq"], sm["st"]
    XN = tl["XN"][:, :]
    psp, ps2, ps3, trb = tl["ps_proj"], tl["ps2"], tl["ps3"], tl["ps_trb"]

    n_pl = 0
    for b in range(1):
        E.dma("y", tl["m_rhs"][b][0:8, :], tl["SELC_dr"], s_yq)
        n_pl = E.dma("y", tl["lhs9"][b][8:9, 0:128], tl["ONES1F_dr"], s_yq)
    E.wait("t", s_yq, n_pl)
    for ek in ("t", "v", "s", "g"):
        E.wait(ek, s_ac, pre["n_ac_final"])
        E.wait(ek, s_dv, pre["n_dv_final"])
    E.wait("g", s_pe, pre.get("wx0_free", 0))
    n_wpre = [0, 0]
    n_wpre[0] = E.dma("g", tl["wx"][0][:, :], dr["W_in_r"][64], s_gq)

    wx_free = [pre.get("wx0_free", 0), pre.get("wx1_free", 0)]
    mrow_free = [0]
    n_conv_prev = 0
    n_silu_prev = 0
    n_em_prev = 0
    n_pevac_prev = 0
    n_Eexp_prev = 0
    n_eg_prev = 0
    n_Mq_prev = 0
    n_Y_prev = 0
    n_xw_prev = 0
    n_folds_prev = 0
    n_yevac_prev = 0
    n_Sdma_prev = 0
    n_ylbdma_prev = 0
    n_xtokevac_prev = 0

    for h in range(nheads):
        b = h % 2
        if h + 1 < nheads:
            bn = (h + 1) % 2
            E.wait("g", s_pe, wx_free[bn])
            n_wpre[bn] = E.dma("g", tl["wx"][bn][:, :], dr["W_in_r"][64 + h + 1], s_gq)
        E.wait("t", s_ac, n_pevac_prev)
        npe_proj, _ = proj_block(E, sm, dr["W_in_r"][64 + h], tl["wx"][b], psp, 0,
                                 XN, O0 - 3, TP, wbuf_free_pe=0,
                                 w_preloaded=n_wpre[b],
                                 ps_free=(s_dv, n_em_prev))
        wx_free[b] = npe_proj
        E.wait("s", s_pe, npe_proj)
        E.wait("s", s_dv, n_conv_prev)
        E.op("s", lambda e: e.copy(tl["xraw_bf"][:, 0:1027], psp[0:128, 0:1027]), s_ac)
        n_pevac = s_ac.n
        n_pevac_prev = n_pevac
        E.wait("v", s_ac, n_pevac)
        E.wait("v", s_pe, n_Y_prev)
        n_silu, n_conv_prev = conv_silu(E, sm, tl["xraw_bf"], tl["cacc"],
                                        tl["xc_bf"], tl["convw"], h)
        n_silu_prev = n_silu
        E.wait("s", s_pe, mrow_free[0])
        E.dma("s", tl["m_rhs"][0][8:9, 0:1024], tl["cums"][h:h + 1, 0:1024], s_sq)
        n_rows = E.dma("s", tl["lhs9"][0][0:8, 0:128], tl["cmlneg"][h:h + 1, 0:1024], s_sq)
        E.wait("t", s_sq, n_rows)
        E.wait("t", s_ac, n_pevac)
        for m in range(2):
            E.op("t", lambda e, m=m: e.matmul(
                psp[0:128, m * 512:(m + 1) * 512], tl["lhs9"][0][0:9, 0:128],
                tl["m_rhs"][0][0:9, m * 512:(m + 1) * 512],
                start=True, stop=True), s_pe)
        n_mMM = s_pe.n
        mrow_free[0] = n_mMM
        E.wait("v", s_pe, n_mMM)
        E.wait("v", s_ac, n_Eexp_prev)
        E.op("v", lambda e: e.tensor_add(tl["em"][:, :], psp[0:128, 0:1024],
                                         tl["MASKC_sb"][:, :]), s_dv)
        n_em = s_dv.n
        n_em_prev = n_em
        E.wait("s", s_dv, n_em)
        E.wait("s", s_po, n_eg_prev)
        E.op("s", lambda e: e.activation(tl["Eh"][:, :], tl["em"][:, :], AF.Exp), s_ac)
        n_Eexp = s_ac.n
        n_Eexp_prev = n_Eexp
        E.wait("g", s_ac, n_Eexp)
        E.wait("g", s_dv, n_Mq_prev)
        E.op("g", lambda e: e.tensor_mul(tl["eg_bf"][:, :], tl["Eh"][:, :],
                                         tl["G_all"][:, :]), s_po)
        n_eg = s_po.n
        n_eg_prev = n_eg
        E.wait("v", s_po, n_eg)
        E.wait("v", s_pe, n_Y_prev)
        E.op("v", lambda e, h=h: e.scalar_tensor_tensor(
            tl["Mq"][:, :], tl["IDENT8_sb"][:, :], tl["Dbc"][:, h:h + 1],
            tl["eg_bf"][:, :], ALU.mult, ALU.add), s_dv)
        n_Mq = s_dv.n
        n_Mq_prev = n_Mq
        E.wait("t", s_ac, n_silu)
        E.wait("t", s_ac, n_xtokevac_prev)
        for c in range(8):
            E.op("t", lambda e, c=c: e.transpose(
                trb[:, c * 128:(c + 1) * 128], tl["xc_bf"][:, c * 128:(c + 1) * 128],
                tl["IDBF_sb"][:, :]), s_pe)
        n_xtr = s_pe.n
        E.wait("s", s_pe, n_xtr)
        E.wait("s", s_dv, n_xw_prev)
        E.wait("s", s_pe, n_Y_prev)
        E.op("s", lambda e: e.copy(tl["x_tok"][:, :], trb[:, :]), s_ac)
        n_xtokevac = s_ac.n
        E.wait("v", s_ac, n_xtokevac)
        E.wait("v", s_pe, n_Y_prev)
        for c in range(8):
            E.op("v", lambda e, c=c, h=h: e.tensor_scalar(
                tl["xw"][:, c * 128:(c + 1) * 128], tl["x_tok"][:, c * 128:(c + 1) * 128],
                tl["wdt_tok"][:, c * 64 + h:c * 64 + h + 1], None, ALU.mult), s_dv)
        n_xw = s_dv.n
        n_xw_prev = n_xw
        E.wait("t", s_dv, n_xw)
        E.wait("t", s_dv, n_yevac_prev)
        for c in range(8):
            E.op("t", lambda e, c=c: e.matmul(
                ps2[:, c * 128:(c + 1) * 128], tl["B_tok"][:, c * 128:(c + 1) * 128],
                tl["xw"][:, c * 128:(c + 1) * 128], start=True, stop=True), s_pe)
        n_ct = s_pe.n
        E.wait("v", s_pe, n_ct)
        E.wait("v", s_st, n_Sdma_prev)
        E.wait("v", s_pe, n_Y_prev)
        E.op("v", lambda e: e.tensor_copy(tl["S_bf"][:, 0:128], ps2[:, 0:128]), s_dv)
        for c in range(1, 8):
            E.wait("v", s_dv)
            E.op("v", lambda e, c=c, h=h: e.scalar_tensor_tensor(
                tl["S_bf"][:, c * 128:(c + 1) * 128],
                tl["S_bf"][:, (c - 1) * 128:c * 128],
                tl["DtotB"][:, c * 64 + h:c * 64 + h + 1],
                ps2[:, c * 128:(c + 1) * 128], ALU.mult, ALU.add), s_dv)
        n_folds = s_dv.n
        n_folds_prev = n_folds
        E.wait("g", s_dv, n_folds)
        n_Sdma_prev = E.dma("g", dr["S_pack"][h * 128:(h + 1) * 128, :],
                            tl["S_bf"][:, 896:1024], s_st)
        E.wait("t", s_dv, n_Mq)
        E.wait("t", s_dv, n_yevac_prev)
        for c in range(8):
            E.op("t", lambda e, c=c: e.matmul(
                ps3[:, c * 128:(c + 1) * 128], tl["Mq"][:, c * 128:(c + 1) * 128],
                tl["x_tok"][:, c * 128:(c + 1) * 128], start=True, stop=True), s_pe)
        E.wait("t", s_dv, n_folds)
        for c in range(1, 8):
            E.op("t", lambda e, c=c: e.matmul(
                ps2[:, c * 128:(c + 1) * 128], tl["Cc"][:, c * 128:(c + 1) * 128],
                tl["S_bf"][:, (c - 1) * 128:c * 128], start=True, stop=True), s_pe)
        n_Y = s_pe.n
        n_Y_prev = n_Y
        E.wait("v", s_pe, n_Y)
        E.op("v", lambda e: e.tensor_copy(tl["ytok_bf"][:, 0:128], ps3[:, 0:128]), s_dv)
        for c in range(1, 8):
            E.op("v", lambda e, c=c, h=h: e.tensor_scalar(
                tl["ytmp"][:, c * 128:(c + 1) * 128], ps2[:, c * 128:(c + 1) * 128],
                tl["expcumstok"][:, c * 64 + h:c * 64 + h + 1], None, ALU.mult), s_dv)
        E.wait("v", s_dv)
        for c in range(1, 8):
            E.op("v", lambda e, c=c: e.tensor_add(
                tl["ytok_bf"][:, c * 128:(c + 1) * 128],
                tl["ytmp"][:, c * 128:(c + 1) * 128],
                ps3[:, c * 128:(c + 1) * 128]), s_dv)
        n_yevac = s_dv.n
        n_yevac_prev = n_yevac
        E.wait("t", s_dv, n_yevac)
        E.wait("t", s_ac, n_xtokevac)
        for c in range(8):
            E.op("t", lambda e, c=c: e.transpose(
                trb[:, c * 128:(c + 1) * 128], tl["ytok_bf"][:, c * 128:(c + 1) * 128],
                tl["IDBF_sb"][:, :]), s_pe)
        n_ytr = s_pe.n
        E.wait("s", s_pe, n_ytr)
        E.wait("s", s_st, n_ylbdma_prev)
        E.op("s", lambda e: e.copy(tl["ylb"][:, :], trb[:, :]), s_ac)
        n_ylbevac = s_ac.n
        n_xtokevac_prev = n_ylbevac
        E.wait("g", s_ac, n_ylbevac)
        n_ylbdma_prev = E.dma("g", dr["ylocal"][h * 128:(h + 1) * 128, :],
                              tl["ylb"][:, :], s_st)
    E.wait("g", s_dv, pre["n_cums"])
    E.dma("g", dr["tot_pack"], tl["coretot"][0:64, 0:1], s_st)
    return dict(wx0_free=wx_free[0], wx1_free=wx_free[1],
                n_pe=s_pe.n, n_dv=s_dv.n, n_ac=s_ac.n, n_st=s_st.n)


def make_tiles_l0_persist(nc, ctx, c, l0_carry=True):
    sb = lambda n, s, d: ctx.enter_context(nc.sbuf_tensor(n, s, d))
    ps = lambda n, s, d: ctx.enter_context(nc.psum_tensor(n, s, d))
    tl = {}
    tl["XN"] = sb("XN", [128, 32 * TP], BF16)
    tl["IDBF_sb"] = sb("IDBF_sb", [128, 128], BF16)
    tl["IDF32_sb"] = sb("IDF32_sb", [128, 128], F32)
    tl["ONES1F_sb"] = sb("ONES1F_sb", [1, 128], F32)
    tl["MASKC_sb"] = sb("MASKC_sb", [128, 1024], F32)
    tl["IDENT8_sb"] = sb("IDENT8_sb", [128, 1024], BF16)
    tl["ZERO64_sb"] = sb("ZERO64_sb", [64, 128], F32)
    tl["ONESCOLB_sb"] = sb("ONESCOLB_sb", [128, 1], BF16)
    tl["EXPAND16_sb"] = sb("EXPAND16_sb", [8, 128], F32)
    if l0_carry:
        tl["Rstd"] = sb("Rstd", [128, 1024], F32)
        tl["wh0"] = sb("wh0", [128, 1024], BF16)
        tl["h0f"] = sb("h0f", [128, 1024], F32)
        tl["nw"] = sb("nw", [128, 64], F32)
    tl["ps_proj"] = ps("ps_proj", [128, 1536], F32)
    tl["ps_trb"] = ps("ps_trb", [128, 1024], BF16)
    tl["ps2"] = ps("ps2", [128, 1024], F32)
    tl["ps3"] = ps("ps3", [128, 1024], F32)
    tl["SELC_dr"] = c["SELC"].ap()
    tl["ONES1F_dr"] = c["ONES1F"].ap()
    return tl


def load_shared_consts(E, tl, sm, c, csc_tile, csc_dram):
    n = E.dma("y", csc_tile[:, :], csc_dram.ap(), sm["yq"])
    for nm, key in (("IDBF_sb", "IDBF"), ("IDF32_sb", "IDF32"), ("ONES1F_sb", "ONES1F"),
                    ("MASKC_sb", "MASKC"), ("IDENT8_sb", "IDENT8"),
                    ("ZERO64_sb", "ZERO64"), ("ONESCOLB_sb", "ONESCOLB"),
                    ("EXPAND16_sb", "EXPAND16")):
        n = E.dma("y", tl[nm][:, :], c[key].ap(), sm["yq"])
    for ek in ("t", "v", "s", "g"):
        E.wait(ek, sm["yq"], n)


def make_tiles_mamba(nc, ctx):
    sb = lambda n, s, d: ctx.enter_context(nc.sbuf_tensor(n, s, d))
    tl = {}
    tl["wx"] = [sb("wx0", [128, 4096], BF16), sb("wx1", [128, 4096], BF16)]
    tl["xraw_bf"] = sb("xraw_bf", [128, 1027], BF16)
    tl["cacc"] = [sb("cacc0", [128, 1024], F32), sb("cacc1", [128, 1024], F32)]
    tl["xc_bf"] = sb("xc_bf", [128, 1024], BF16)
    tl["convw"] = sb("sb_convw", [128, 264], F32)
    tl["Bc"] = sb("Bc", [128, 1024], BF16)
    tl["Cc"] = sb("Cc", [128, 1024], BF16)
    tl["CcF"] = sb("CcF", [128, 1024], F32)
    tl["B_tok"] = sb("B_tok", [128, 1024], BF16)
    tl["G_all"] = sb("G_all", [128, 1024], BF16)
    tl["dtb"] = sb("sb_dtb", [64, 1], F32)
    tl["alog"] = sb("sb_alog", [64, 1], F32)
    tl["dpar"] = sb("sb_dpar", [64, 1], F32)
    tl["dtraw"] = sb("dtraw", [64, 1024], F32)
    tl["spa"] = sb("spa", [64, 1024], F32)
    tl["spb"] = sb("spb", [64, 1024], F32)
    tl["dtsp"] = sb("dtsp", [64, 1024], F32)
    tl["logdA"] = sb("logdA", [64, 1024], F32)
    tl["cums"] = sb("cums", [64, 1024], F32)
    tl["cmlneg"] = sb("cmlneg", [64, 1024], F32)
    tl["ecc"] = sb("ecc", [64, 1024], F32)
    tl["wdtT"] = sb("wdtT", [64, 1024], F32)
    tl["dcol"] = sb("dcol", [64, 2], F32)
    tl["ptot"] = sb("ptot", [64, 8], F32)
    tl["totm"] = sb("totm", [64, 8], F32)
    tl["totT"] = sb("totT", [8, 64], F32)
    tl["coretot"] = sb("coretot", [64, 1], F32)
    tl["rowf"] = sb("rowf", [1, 64], F32)
    tl["rowf2"] = sb("rowf2", [1, 512], F32)
    tl["cumstok"] = sb("cumstok", [128, 512], F32)
    tl["expcumstok"] = sb("expcumstok", [128, 512], F32)
    tl["wdt_tok"] = sb("wdt_tok", [128, 512], F32)
    tl["Dbc"] = sb("Dbc", [128, 64], F32)
    tl["DtotB"] = sb("DtotB", [128, 512], F32)
    tl["m_rhs"] = [sb("m_rhs0", [9, 1024], F32)]
    tl["lhs9"] = [sb("lhs90", [9, 128], F32)]
    tl["em"] = sb("em", [128, 1024], F32)
    tl["Eh"] = sb("Eh", [128, 1024], BF16)
    tl["eg_bf"] = sb("eg_bf", [128, 1024], BF16)
    tl["Mq"] = sb("Mq", [128, 1024], BF16)
    tl["x_tok"] = sb("x_tok", [128, 1024], BF16)
    tl["xw"] = sb("xw", [128, 1024], BF16)
    tl["S_bf"] = sb("S_bf", [128, 1024], BF16)
    tl["ytmp"] = sb("ytmp", [128, 1024], F32)
    tl["ytok_bf"] = sb("ytok_bf", [128, 1024], BF16)
    tl["ylb"] = sb("ylb", [128, 1024], BF16)
    return tl


def make_tiles_phase_c(tl, nc, ctx):
    sb = lambda n, s, d: ctx.enter_context(nc.sbuf_tensor(n, s, d))
    tl["sz_bf"] = tl["Mq"]
    tl["SS"] = tl["x_tok"]
    tl["Cbar"] = tl["eg_bf"]
    tl["ylb2"] = tl["ytok_bf"]
    tl["t1f"] = tl["ytmp"]
    tl["sqb"] = tl["xw"]
    tl["H_bf"] = tl["S_bf"]
    tl["Hacc"] = tl["cacc"]
    tl["Wbig"] = tl["cumstok"]
    tl["totg"] = sb("totg", [64, 8], F32)
    tl["maskb"] = sb("maskb", [64, 8], F32)
    tl["mt8"] = sb("mt8", [64, 8], F32)
    tl["R8"] = sb("R8", [64, 8], F32)
    tl["W8"] = sb("W8", [64, 8], F32)
    tl["rkm"] = sb("rkm", [8, 1], F32)
    tl["lg0"] = tl["dtraw"]
    tl["ltok"] = sb("ltok", [128, 64], F32)
    tl["gtok"] = sb("gtok", [128, 64], F32)
    tl["gmax"] = sb("gmax", [128, 1], F32)
    tl["g0"] = tl["logdA"]
    tl["erow"] = tl["spb"][0:1, 0:1024].tensor
    tl["erow"] = tl["spb"]
    tl["rowW"] = tl["rowf2"]
    tl["sqacc"] = tl["spa"]
    return tl


def build_mamba_phase_c(E, tl, sm, dr, pre, nheads=64):
    s_pe, s_dv, s_ac, s_po = sm["pe"], sm["dv"], sm["ac"], sm["po"]
    s_gq, s_sq, s_yq, s_st = sm["gq"], sm["sq"], sm["yq"], sm["st"]
    XN = tl["XN"][:, :]
    psp, ps2, ps3 = tl["ps_proj"], tl["ps2"], tl["ps3"]
    for ek in ("t", "v", "s", "g", "y"):
        E.wait(ek, s_ac, pre["n_ac_final"])
        E.wait(ek, s_dv, pre["n_dv_final"])

    n_rk = E.dma("y", tl["rkm"][:, :], dr["rkmask"], s_yq)
    n_tg = E.dma("y", tl["totT"][0:8, 0:64],
                 dr["tot_all"].rearrange("(r h) o -> r (h o)", r=8), s_yq)
    E.wait("t", s_yq, n_tg)
    E.op("t", lambda e: e.transpose(ps3[0:64, 8:16], tl["totT"][0:8, 0:64],
                                    tl["IDF32_sb"][0:8, 0:8]), s_pe)
    E.wait("v", s_pe)
    E.op("v", lambda e: e.tensor_copy(tl["totg"][:, :], ps3[0:64, 8:16]), s_dv)
    E.op("t", lambda e: e.transpose(ps3[0:1, 0:8], tl["rkm"][0:8, 0:1],
                                    tl["IDF32_sb"][0:8, 0:8]), s_pe)
    E.wait("s", s_pe)
    E.op("s", lambda e: e.copy(tl["rowf"][0:1, 0:8], ps3[0:1, 0:8]), s_ac)
    E.wait("t", s_ac)
    E.op("t", lambda e: e.matmul(ps3[0:64, 0:8], tl["ONES1F_sb"][0:1, 0:64],
                                 tl["rowf"][0:1, 0:8], start=True, stop=True), s_pe)
    E.wait("v", s_pe)
    E.op("v", lambda e: e.tensor_copy(tl["maskb"][:, :], ps3[0:64, 0:8]), s_dv)
    E.wait("v", s_dv)
    E.op("v", lambda e: e.tensor_mul(tl["mt8"][:, :], tl["totg"][:, :],
                                     tl["maskb"][:, :]), s_dv)
    E.wait("v", s_dv)
    E.op("v", lambda e: e.tensor_copy(tl["R8"][:, 0:1], tl["mt8"][:, 0:1]), s_dv)
    for j in range(1, 8):
        E.wait("v", s_dv)
        E.op("v", lambda e, j=j: e.tensor_add(
            tl["R8"][:, j:j + 1], tl["R8"][:, j - 1:j], tl["mt8"][:, j:j + 1]), s_dv)
    E.wait("v", s_dv)
    E.op("v", lambda e: e.tensor_scalar(
        tl["W8"][:, :], tl["R8"][:, :], tl["R8"][:, 7:8], None, ALU.subtract), s_dv)
    E.wait("s", s_dv)
    E.op("s", lambda e: e.activation(tl["totg"][:, :], tl["W8"][:, :], AF.Exp,
                                     scale=-1.0), s_ac)
    E.wait("v", s_ac)
    E.op("v", lambda e: e.tensor_mul(tl["mt8"][:, :], tl["totg"][:, :],
                                     tl["maskb"][:, :]), s_dv)
    n_w3 = s_dv.n
    E.wait("y", s_dv, n_w3)
    n_fl = E.dma("y", tl["rowW"][0:1, 0:512], tl["mt8"][0:64, 0:8], s_yq)
    E.wait("t", s_yq, n_fl)
    E.op("t", lambda e: e.matmul(ps3[0:128, 0:512], tl["ONES1F_sb"][0:1, :],
                                 tl["rowW"][0:1, 0:512], start=True, stop=True), s_pe)
    E.wait("v", s_pe)
    E.op("v", lambda e: e.tensor_copy(tl["Wbig"][:, :], ps3[0:128, 0:512]), s_dv)
    n_wbig = s_dv.n
    E.op("v", lambda e: e.memset(tl["sqacc"][:, :], 0.0), s_dv)

    SSv = dr["S_all"].rearrange("(j h n) p -> h n j p", j=8, h=64)

    wx_free = [pre.get("wx0_free", 0), pre.get("wx1_free", 0)]
    n_wpre = [0, 0]
    n_silu_prev = 0
    n_t1_prev = 0
    n_yg_dma_prev = 0
    n_cbar_prev = 0
    n_eb_prev = 0
    n_ycorr_prev = 0
    n_hfold_prev = 0
    n_ygmul_prev = 0
    n_sqmm_prev = 0
    n_sqact_prev = 0
    n_sqadd_prev = 0
    E.wait("g", s_pe, wx_free[0])
    n_wpre[0] = E.dma("g", tl["wx"][0][:, :], dr["W_in_r"][0], s_gq)
    wx_free[0] = 0

    for h in range(nheads):
        b = h % 2
        if h + 1 < nheads:
            bn = (h + 1) % 2
            E.wait("g", s_pe, wx_free[bn])
            n_wpre[bn] = E.dma("g", tl["wx"][bn][:, :], dr["W_in_r"][h + 1], s_gq)
            wx_free[bn] = 0
        E.wait("t", s_ac, n_silu_prev)
        npe_z, _ = proj_block(E, sm, None, tl["wx"][b], psp, 0, XN, O0, TP,
                              w_preloaded=n_wpre[b])
        wx_free[b] = npe_z
        E.wait("s", s_pe, npe_z)
        E.wait("s", s_dv, n_ygmul_prev)
        E.op("s", lambda e: e.activation(tl["sz_bf"][:, 0:1024], psp[0:128, 0:1024],
                                         AF.Silu), s_ac)
        n_silu = s_ac.n
        n_silu_prev = n_silu
        E.wait("y", s_dv, n_hfold_prev)
        n_ss = E.dma("y", tl["SS"][:, 0:1024], SSv[h], s_yq)
        E.wait("v", s_yq, n_ss)
        E.wait("v", s_dv, n_wbig)
        E.wait("v", s_pe, n_ycorr_prev)
        E.op("v", lambda e, h=h: e.tensor_scalar(
            tl["Hacc"][0][:, 0:128], tl["SS"][:, 0:128], tl["Wbig"][:, h * 8:h * 8 + 1],
            None, ALU.mult), s_dv)
        for j in range(1, 7):
            E.wait("v", s_dv)
            E.op("v", lambda e, j=j, h=h: e.scalar_tensor_tensor(
                tl["Hacc"][j % 2][:, 0:128], tl["SS"][:, j * 128:(j + 1) * 128],
                tl["Wbig"][:, h * 8 + j:h * 8 + j + 1], tl["Hacc"][(j + 1) % 2][:, 0:128],
                ALU.mult, ALU.add), s_dv)
        E.wait("v", s_dv)
        E.op("v", lambda e, h=h: e.scalar_tensor_tensor(
            tl["H_bf"][:, 0:128], tl["SS"][:, 7 * 128:8 * 128],
            tl["Wbig"][:, h * 8 + 7:h * 8 + 8], tl["Hacc"][0][:, 0:128],
            ALU.mult, ALU.add), s_dv)
        n_hfold = s_dv.n
        n_hfold_prev = n_hfold
        E.wait("s", s_pe, n_eb_prev)
        E.dma("s", tl["erow"][0:1, 0:1024], tl["ecc"][h:h + 1, 0:1024], s_sq)
        n_er = s_sq.n
        E.wait("t", s_sq, n_er)
        E.wait("t", s_dv, n_cbar_prev)
        for m in range(2):
            E.op("t", lambda e, m=m: e.matmul(
                ps2[0:128, m * 512:(m + 1) * 512], tl["ONES1F_sb"][0:1, :],
                tl["erow"][0:1, m * 512:(m + 1) * 512], start=True, stop=True), s_pe)
        n_eb = s_pe.n
        n_eb_prev = n_eb
        E.wait("v", s_pe, n_eb)
        E.wait("v", s_pe, n_ycorr_prev)
        E.op("v", lambda e: e.tensor_mul(tl["Cbar"][:, 0:1024], tl["CcF"][:, :],
                                         ps2[0:128, 0:1024]), s_dv)
        n_cbar = s_dv.n
        n_cbar_prev = n_cbar
        E.wait("t", s_dv, n_cbar)
        E.wait("t", s_dv, n_hfold)
        E.wait("t", s_dv, n_t1_prev)
        for m in range(2):
            E.op("t", lambda e, m=m: e.matmul(
                ps3[0:128, m * 512:(m + 1) * 512], tl["H_bf"][:, 0:128],
                tl["Cbar"][:, m * 512:(m + 1) * 512], start=True, stop=True), s_pe)
        n_yc = s_pe.n
        n_ycorr_prev = n_yc
        E.wait("y", s_dv, n_t1_prev)
        n_yl = E.dma("y", tl["ylb2"][:, 0:1024], dr["ylocal"][h * 128:(h + 1) * 128, :],
                     s_yq)
        E.wait("v", s_yq, n_yl)
        E.wait("v", s_pe, n_yc)
        E.op("v", lambda e: e.tensor_add(tl["t1f"][:, 0:1024], tl["ylb2"][:, 0:1024],
                                         ps3[0:128, 0:1024]), s_dv)
        n_t1x = s_dv.n
        n_t1_prev = n_t1x
        E.wait("v", s_ac, n_silu)
        E.wait("v", s_st, n_yg_dma_prev)
        E.wait("v", s_ac, n_sqact_prev)
        E.op("v", lambda e: e.tensor_mul(tl["ylb"][:, 0:1024], tl["t1f"][:, 0:1024],
                                         tl["sz_bf"][:, 0:1024]), s_dv)
        n_ygmul = s_dv.n
        n_ygmul_prev = n_ygmul
        E.wait("g", s_dv, n_ygmul)
        n_yg_dma_prev = E.dma("g", dr["yg"][h * 128:(h + 1) * 128, :],
                              tl["ylb"][:, 0:1024], s_st)
        E.wait("s", s_dv, n_ygmul)
        E.wait("s", s_pe, n_sqmm_prev)
        E.op("s", lambda e: e.activation(tl["sqb"][:, 0:1024], tl["ylb"][:, 0:1024],
                                         AF.Square), s_ac)
        n_sqact_prev = s_ac.n
        E.wait("t", s_ac, n_sqact_prev)
        E.wait("t", s_dv, n_sqadd_prev)
        E.op("t", lambda e: e.matmul(psp[0:1, 1024:1536], tl["ONESCOLB_sb"][:, 0:1],
                                     tl["sqb"][:, 0:512], start=True, stop=True), s_pe)
        E.op("t", lambda e: e.matmul(psp[32:33, 1024:1536], tl["ONESCOLB_sb"][:, 0:1],
                                     tl["sqb"][:, 512:1024], start=True, stop=True), s_pe)
        n_sqmm = s_pe.n
        n_sqmm_prev = n_sqmm
        E.wait("v", s_pe, n_sqmm)
        E.wait("v", s_dv)
        E.op("v", lambda e: e.tensor_add(tl["sqacc"][0:1, 0:512], tl["sqacc"][0:1, 0:512],
                                         psp[0:1, 1024:1536]), s_dv)
        E.wait("v", s_dv)
        E.op("v", lambda e: e.tensor_add(tl["sqacc"][0:1, 512:1024],
                                         tl["sqacc"][0:1, 512:1024],
                                         psp[32:33, 1024:1536]), s_dv)
        n_sqadd_prev = s_dv.n

    E.wait("s", s_dv, n_sqadd_prev)
    E.op("s", lambda e: e.activation(tl["erow"][0:1, 0:1024], tl["sqacc"][0:1, 0:1024],
                                     AF.Sqrt, bias=float(EPS), scale=float(1.0 / DSSM)),
         s_ac)
    E.wait("v", s_ac)
    E.op("v", lambda e: e.reciprocal(tl["sqacc"][0:1, 0:1024], tl["erow"][0:1, 0:1024]),
         s_dv)
    n_rr = s_dv.n
    E.wait("t", s_dv, n_rr)
    E.wait("t", s_dv, n_t1_prev)
    for m in range(2):
        E.op("t", lambda e, m=m: e.matmul(
            ps3[0:128, m * 512:(m + 1) * 512], tl["ONES1F_sb"][0:1, :],
            tl["sqacc"][0:1, m * 512:(m + 1) * 512], start=True, stop=True), s_pe)
    n_rb = s_pe.n
    E.wait("v", s_pe, n_rb)
    E.wait("v", s_ac, n_sqact_prev)
    E.op("v", lambda e: e.tensor_copy(tl["Rstd"][:, 0:1024], ps3[0:128, 0:1024]), s_dv)
    n_rstd = s_dv.n
    return dict(n_rstd=n_rstd, n_ac=s_ac.n, n_dv=s_dv.n, n_pe=s_pe.n,
                wx0_free=wx_free[0], wx1_free=wx_free[1])


def make_tiles_l0tail(tl, nc, ctx):
    sb = lambda n, s, d: ctx.enter_context(nc.sbuf_tensor(n, s, d))
    tl["ynb"] = tl["XN"]
    tl["wob"] = [sb("wob0", [128, 8192], BF16), sb("wob1", [128, 8192], BF16)]
    tl["ygl"] = [sb("ygl0", [128, 512], BF16), sb("ygl1", [128, 512], BF16)]
    tl["bw0b"] = sb("bw0b", [128, 4096], BF16)
    tl["obuf"] = sb("obuf", [128, 512], BF16)
    return tl


def make_tiles_resid(tl, nc, ctx):
    sb = lambda n, s, d: ctx.enter_context(nc.sbuf_tensor(n, s, d))
    tl["mfb"] = sb("mfb", [128, 4096], BF16)
    tl["x1b"] = sb("x1b", [128, 4096], BF16)
    tl["xot"] = sb("xot", [128, 4096], F32)
    tl["x1f"] = sb("x1f", [128, 4096], F32)
    return tl


def build_ffn0_pre(E, tl, sm, dr, cinfo):
    s_pe, s_dv, s_ac, s_po = sm["pe"], sm["dv"], sm["ac"], sm["po"]
    s_gq, s_sq, s_yq, s_st = sm["gq"], sm["sq"], sm["yq"], sm["st"]
    for ek in ("t", "v", "s", "g", "y"):
        E.wait(ek, s_ac, cinfo["n_ac"])
        E.wait(ek, s_dv, cinfo["n_dv"])
        E.wait(ek, s_pe, cinfo["n_pe"])
    XN = tl["XN"][:, :]
    psp, ps2, ps3, trb = tl["ps_proj"], tl["ps2"], tl["ps3"], tl["ps_trb"]

    E.wait("t", s_ac, cinfo["n_ac"])
    npe_h0, _ = proj_block(E, sm, dr["A0_r"], tl["wx"][0], psp, 0, XN, O0, TP,
                           wbuf_free_pe=cinfo["wx0_free"])
    E.wait("s", s_pe, npe_h0)
    E.op("s", lambda e: e.copy(tl["h0f"][:, :], psp[0:128, 0:1024]), s_ac)
    n_h0 = s_ac.n
    npe_lg, _ = proj_block(E, sm, dr["routeT"], tl["wx"][1], psp, 0, XN, O0, TP,
                           tile_w=8, wbuf_free_pe=cinfo["wx1_free"],
                           ps_free=(s_ac, n_h0))
    E.wait("s", s_pe, npe_lg)
    E.op("s", lambda e: e.copy(tl["lg0"][0:8, 0:1024], psp[0:8, 0:1024]), s_ac)
    n_lg = s_ac.n
    E.wait("t", s_ac, n_lg)
    E.wait("t", s_dv, cinfo["n_dv"])
    for c in range(8):
        E.op("t", lambda e, c=c: e.transpose(
            ps2[0:128, c * 8:(c + 1) * 8], tl["lg0"][0:8, c * 128:(c + 1) * 128],
            tl["IDF32_sb"][0:8, 0:8]), s_pe)
    n_tr1 = s_pe.n
    E.wait("v", s_pe, n_tr1)
    E.op("v", lambda e: e.tensor_copy(tl["ltok"][:, :], ps2[0:128, 0:64]), s_dv)
    for c in range(8):
        E.wait("v", s_dv)
        E.op("v", lambda e, c=c: e.tensor_reduce(
            tl["gmax"][:, 0:1], tl["ltok"][:, c * 8:(c + 1) * 8], AX.X, ALU.max), s_dv)
        E.wait("v", s_dv)
        E.op("v", lambda e, c=c: e.tensor_scalar(
            tl["gtok"][:, c * 8:(c + 1) * 8], tl["ltok"][:, c * 8:(c + 1) * 8],
            tl["gmax"][:, 0:1], None, ALU.is_ge), s_dv)
        E.wait("v", s_dv)
        E.op("v", lambda e, c=c: e.tensor_mul(
            tl["gtok"][:, c * 8:(c + 1) * 8], tl["gtok"][:, c * 8:(c + 1) * 8],
            tl["ltok"][:, c * 8:(c + 1) * 8]), s_dv)
    n_gt = s_dv.n
    E.wait("t", s_dv, n_gt)
    for c in range(8):
        E.op("t", lambda e, c=c: e.transpose(
            ps2[0:8, c * 128:(c + 1) * 128], tl["gtok"][0:128, c * 8:(c + 1) * 8],
            tl["IDF32_sb"][:, :]), s_pe)
    n_tr2 = s_pe.n
    E.wait("v", s_pe, n_tr2)
    E.op("v", lambda e: e.tensor_copy(tl["g0"][0:8, 0:1024], ps2[0:8, 0:1024]), s_dv)
    n_g0 = s_dv.n
    E.wait("t", s_dv, n_g0)
    E.wait("t", s_dv, cinfo["n_rstd"])
    for m in range(2):
        E.op("t", lambda e, m=m: e.matmul(
            ps3[0:128, m * 512:(m + 1) * 512], tl["EXPAND16_sb"][0:8, :],
            tl["g0"][0:8, m * 512:(m + 1) * 512], start=True, stop=True), s_pe)
    n_ex = s_pe.n
    E.wait("v", s_pe, n_ex)
    E.wait("v", s_ac, n_h0)
    E.op("v", lambda e: e.tensor_mul(tl["wh0"][:, :], tl["h0f"][:, :],
                                     ps3[0:128, 0:1024]), s_dv)
    n_wh0 = s_dv.n

    return dict(n_wh0=n_wh0, n_pe=s_pe.n, n_dv=s_dv.n, n_ac=s_ac.n,
                n_rstd=cinfo["n_rstd"])


def build_wout(E, tl, sm, dr, winfo):
    s_pe, s_dv, s_ac, s_po = sm["pe"], sm["dv"], sm["ac"], sm["po"]
    s_gq, s_sq, s_yq, s_st = sm["gq"], sm["sq"], sm["yq"], sm["st"]
    psp, ps2, ps3, trb = tl["ps_proj"], tl["ps2"], tl["ps3"], tl["ps_trb"]
    n_wh0 = winfo["n_wh0"]
    cinfo = winfo
    n_nw = E.dma("y", tl["nw"][:, :], dr["nw_r"], s_yq)
    n_bw = E.dma("g", tl["bw0b"][:, :], dr["Bw0_r"], s_gq)
    n_wo_pre = [0, 0]
    wob_free = [0, 0]
    ygl_free = [0, 0]
    n_ynmm_prev = 0
    n_obuf_st_prev = 0
    n_ps_free_ac = [0, 0]
    n_wo_pre[0] = E.dma("g", tl["wob"][0][:, :], dr["W_out_r"][0], s_gq)
    E.wait("t", s_dv, n_wh0)
    for half in range(2):
        for ch in range(64):
            bb = ch % 2
            E.wait("y", s_dv, ygl_free[bb])
            n_yg = E.dma("y", tl["ygl"][bb][:, :],
                         dr["yg"][ch * 128:(ch + 1) * 128, half * 512:(half + 1) * 512],
                         s_yq)
            E.wait("v", s_yq, n_yg)
            if half == 0 and ch == 0:
                E.wait("v", s_yq, n_nw)
                E.wait("v", s_dv, cinfo["n_rstd"])
            E.wait("v", s_pe, n_ynmm_prev)
            E.op("v", lambda e, ch=ch, bb=bb, half=half: e.scalar_tensor_tensor(
                tl["ynb"][:, ch * 512:(ch + 1) * 512], tl["ygl"][bb][:, :],
                tl["nw"][:, ch:ch + 1],
                tl["Rstd"][:, half * 512:(half + 1) * 512], ALU.mult, ALU.mult), s_dv)
            ygl_free[bb] = s_dv.n
        n_ynb = s_dv.n
        for dt_ in range(32):
            bb = dt_ % 2
            gidx = half * 32 + dt_
            if gidx + 1 < 64:
                bn = (gidx + 1) % 2
                E.wait("g", s_pe, wob_free[bn])
                n_wo_pre[bn] = E.dma("g", tl["wob"][bn][:, :],
                                     dr["W_out_r"][(gidx + 1) % 32], s_gq)
            E.wait("t", s_gq, n_wo_pre[bb])
            E.wait("t", s_dv, n_ynb)
            E.wait("t", s_ac, n_ps_free_ac[bb])
            psl = ps2[:, bb * 512:(bb + 1) * 512]
            for ch in range(64):
                E.op("t", lambda e, ch=ch, bb=bb, pl=psl: e.matmul(
                    pl, tl["wob"][bb][:, ch * 128:(ch + 1) * 128],
                    tl["ynb"][:, ch * 512:(ch + 1) * 512],
                    start=(ch == 0), stop=False), s_pe)
            if gidx == 0:
                E.wait("t", s_gq, n_bw)
                E.wait("t", s_dv, n_wh0)
            E.op("t", lambda e, dt_=dt_, pl=psl, half=half: e.matmul(
                pl, tl["bw0b"][:, dt_ * 128:(dt_ + 1) * 128],
                tl["wh0"][:, half * 512:(half + 1) * 512],
                start=False, stop=True), s_pe)
            wob_free[bb] = s_pe.n
            n_mm = s_pe.n
            E.wait("s", s_pe, n_mm)
            E.wait("s", s_st, n_obuf_st_prev)
            E.op("s", lambda e, pl=psl: e.copy(tl["obuf"][:, :], pl), s_ac)
            n_ob = s_ac.n
            n_ps_free_ac[bb] = n_ob
            E.wait("g", s_ac, n_ob)
            n_obuf_st_prev = E.dma(
                "g", dr["mf_out"][dt_ * 128:(dt_ + 1) * 128,
                                  half * 512:(half + 1) * 512],
                tl["obuf"][:, :], s_st)
        n_ynmm_prev = s_pe.n
    return dict(n_pe=s_pe.n, n_dv=s_dv.n, n_ac=s_ac.n, n_st=s_st.n)


def build_l0_resid(E, tl, sm, dr, winfo):
    s_pe, s_dv, s_ac = sm["pe"], sm["dv"], sm["ac"]
    s_gq, s_yq, s_st = sm["gq"], sm["yq"], sm["st"]
    trb = tl["ps_trb"]
    mfv = dr["mf_out"].rearrange("(dt p) t -> p dt t", dt=32)
    n_mf_free = 0
    n_x1b_free = 0
    n_xot_free = 0
    n_x1f_free = 0
    for tt in range(8):
        E.wait("y", s_pe, n_mf_free)
        n_mf = E.dma("y", tl["mfb"][:, :], mfv[:, :, tt * 128:(tt + 1) * 128], s_yq)
        E.wait("y", s_dv, n_xot_free)
        n_xo = E.dma("y", tl["xot"][:, :],
                     dr["x"][O0 + tt * 128:O0 + (tt + 1) * 128, :], s_yq)
        for g4 in range(4):
            E.wait("t", s_yq, n_mf)
            E.wait("t", s_ac)
            E.wait("t", s_dv, n_x1b_free)
            for j in range(8):
                dt_ = g4 * 8 + j
                E.op("t", lambda e, dt_=dt_, j=j: e.transpose(
                    trb[:, j * 128:(j + 1) * 128],
                    tl["mfb"][:, dt_ * 128:(dt_ + 1) * 128],
                    tl["IDBF_sb"][:, :]), s_pe)
            n_tr = s_pe.n
            E.wait("s", s_pe, n_tr)
            E.op("s", lambda e, g4=g4: e.copy(
                tl["x1b"][:, g4 * 1024:(g4 + 1) * 1024], trb[:, 0:1024]), s_ac)
        n_mf_free = s_pe.n
        n_ev = sm["ac"].n
        E.wait("v", s_ac, n_ev)
        E.wait("v", s_yq, n_xo)
        E.wait("v", s_st, n_x1f_free)
        E.op("v", lambda e: e.tensor_add(tl["x1f"][:, :], tl["x1b"][:, :],
                                         tl["xot"][:, :]), s_dv)
        n_a = s_dv.n
        n_xot_free = n_a
        n_x1b_free = n_a
        E.wait("g", s_dv, n_a)
        n_x1f_free = E.dma("g", dr["x1"][tt * 128:(tt + 1) * 128, :], tl["x1f"][:, :],
                           s_st)
    return dict(n_st=s_st.n, n_pe=s_pe.n, n_dv=s_dv.n, n_ac=s_ac.n)


BATCHER8 = [
    [(0, 1), (2, 3), (4, 5), (6, 7)],
    [(0, 2), (1, 3), (4, 6), (5, 7)],
    [(1, 2), (5, 6)],
    [(0, 4), (1, 5), (2, 6), (3, 7)],
    [(2, 4), (3, 5)],
    [(1, 2), (3, 4), (5, 6)],
]


def make_tiles_l1_persist(tl, nc, ctx):
    sb = lambda n, s, d: ctx.enter_context(nc.sbuf_tensor(n, s, d))
    tl["QE"] = sb("QE", [128, 8192], BF16)
    tl["ATT"] = sb("ATT", [128, 8192], BF16)
    tl["wh1"] = sb("wh1", [128, 1024], BF16)
    tl["h1b"] = sb("h1b", [128, 1024], BF16)
    tl["rrow"] = sb("rrow", [1, 1024], F32)
    return tl


def make_tiles_l1qkv(tl, nc, ctx):
    sb = lambda n, s, d: ctx.enter_context(nc.sbuf_tensor(n, s, d))
    tl["wqb"] = [sb("wqb0", [128, 16384], BF16), sb("wqb1", [128, 16384], BF16)]
    _wkb = sb("wkb0", [128, 4096], BF16)
    tl["wkb"] = [_wkb, _wkb]
    tl["kvtmp"] = sb("kvtmp", [128, 1024], BF16)
    tl["vtok"] = sb("vtok", [128, 1024], BF16)
    tl["lg1"] = sb("lg1", [8, 1024], F32)
    tl["lt1"] = sb("lt1", [128, 64], F32)
    tl["zs1"] = sb("zs1", [128, 64], F32)
    tl["zc1"] = sb("zc1", [128, 64], F32)
    tl["sp_t"] = sb("sp_t", [128, 64], F32)
    tl["sp_i"] = sb("sp_i", [128, 64], F32)
    tl["sp_d"] = sb("sp_d", [128, 64], F32)
    tl["sp_hi"] = sb("sp_hi", [128, 8], F32)
    tl["sp_lo"] = sb("sp_lo", [128, 8], F32)
    tl["kz1"] = sb("kz1", [128, 8], F32)
    tl["tau1"] = sb("tau1", [128, 8], F32)
    tl["w1t"] = sb("w1t", [128, 64], F32)
    tl["w1T"] = sb("w1T", [8, 1024], F32)
    tl["brt"] = sb("sb_brt", [8, 1], F32)
    return tl


def build_l1_qkv(E, tl, sm, dr):
    s_pe, s_dv, s_ac, s_po = sm["pe"], sm["dv"], sm["ac"], sm["po"]
    s_gq, s_sq, s_yq, s_st = sm["gq"], sm["sq"], sm["yq"], sm["st"]
    XN = tl["XN"][:, :]
    psp, ps2, ps3, trb = tl["ps_proj"], tl["ps2"], tl["ps3"], tl["ps_trb"]

    def projq(wbuf, nk, ps, pw, cols=1024):
        for k in range(nk):
            kc = k % 32
            for m in range(cols // 512):
                E.op("t", lambda e, k=k, kc=kc, m=m: e.matmul(
                    ps[0:pw, m * 512:(m + 1) * 512],
                    wbuf[:, k * pw:(k + 1) * pw],
                    XN[:, kc * 1024 + m * 512:kc * 1024 + (m + 1) * 512],
                    start=(k == 0), stop=(k == nk - 1)), s_pe)
        return s_pe.n

    n_wq = [0, 0]
    wq_free = [0, 0]
    n_qe_prev = 0
    n_wq[0] = E.dma("g", tl["wqb"][0][:, :], dr["wq_r"][0], s_gq)
    for kv in range(8):
        b = kv % 2
        if kv + 1 < 8:
            bn = (kv + 1) % 2
            E.wait("g", s_pe, wq_free[bn])
            n_wq[bn] = E.dma("g", tl["wqb"][bn][:, :], dr["wq_r"][kv + 1], s_gq)
        E.wait("t", s_gq, n_wq[b])
        E.wait("t", s_ac, n_qe_prev)
        npe = projq(tl["wqb"][b], 128, ps2, 128)
        wq_free[b] = npe
        E.wait("s", s_pe, npe)
        E.op("s", lambda e, kv=kv: e.copy(
            tl["QE"][:, kv * 1024:(kv + 1) * 1024], ps2[0:128, 0:1024]), s_ac)
        n_qe_prev = s_ac.n
    n_qe = n_qe_prev

    n_wk = [0, 0]
    wk_free = [0, 0]
    n_kv_prev = 0
    n_kdma_prev = 0
    n_vtok_prev = 0
    n_vdma_prev = 0
    for vv in range(2):
        wsrc = dr["wk_r"] if vv == 0 else dr["wv_r"]
        for kv in range(8):
            b = kv % 2
            E.wait("g", s_pe, wk_free[b])
            n_wk[b] = E.dma("g", tl["wkb"][b][:, :], wsrc[kv], s_gq)
            E.wait("t", s_gq, n_wk[b])
            E.wait("t", s_ac, n_kv_prev)
            npe = projq(tl["wkb"][b], 32, ps3, 128)
            wk_free[b] = npe
            E.wait("s", s_pe, npe)
            if vv == 0:
                E.wait("s", s_st, n_kdma_prev)
                E.op("s", lambda e: e.copy(tl["kvtmp"][:, :], ps3[0:128, 0:1024]), s_ac)
                n_kv_prev = s_ac.n
                E.wait("g", s_ac, n_kv_prev)
                n_kdma_prev = E.dma("g", dr["Kpack"][kv * 128:(kv + 1) * 128, :],
                                    tl["kvtmp"][:, :], s_st)
            else:
                E.wait("s", s_pe, n_vtok_prev)
                E.op("s", lambda e: e.copy(tl["kvtmp"][:, :], ps3[0:128, 0:1024]), s_ac)
                n_ev = s_ac.n
                E.wait("t", s_ac, n_ev)
                E.wait("t", s_ac)
                for c in range(8):
                    E.op("t", lambda e, c=c: e.transpose(
                        trb[:, c * 128:(c + 1) * 128],
                        tl["kvtmp"][:, c * 128:(c + 1) * 128],
                        tl["IDBF_sb"][:, :]), s_pe)
                n_vtok_prev = s_pe.n
                E.wait("s", s_pe, n_vtok_prev)
                E.wait("s", s_st, n_vdma_prev)
                E.op("s", lambda e: e.copy(tl["vtok"][:, :], trb[:, :]), s_ac)
                n_vt = s_ac.n
                E.wait("g", s_ac, n_vt)
                n_vdma_prev = E.dma(
                    "g",
                    dr["Vpack"][kv * 1024:(kv + 1) * 1024, :]
                    .rearrange("(cl s) d -> s cl d", cl=8),
                    tl["vtok"][:, :], s_st)

    n_brt = E.dma("y", tl["brt"][:, :], dr["brt"], s_yq)
    npe_h1, _ = proj_block(E, sm, dr["lA1_r"], tl["wkb"][0], psp, 0, XN, 0, 1024,
                           wbuf_free_pe=wk_free[0], slab_w=1024)
    E.wait("s", s_pe, npe_h1)
    E.op("s", lambda e: e.copy(tl["h1b"][:, :], psp[0:128, 0:1024]), s_ac)
    n_h1 = s_ac.n
    npe_lg, _ = proj_block(E, sm, dr["wrt_r"], tl["wkb"][1], psp, 0, XN, 0, 1024,
                           tile_w=8, wbuf_free_pe=wk_free[1], ps_free=(s_ac, n_h1),
                           slab_w=1024)
    E.wait("s", s_pe, npe_lg)
    E.wait("s", s_yq, n_brt)
    E.op("s", lambda e: e.activation(tl["lg1"][0:8, 0:1024], psp[0:8, 0:1024],
                                     AF.Identity, bias=tl["brt"][0:8, 0:1]), s_ac)
    n_lg = s_ac.n
    E.wait("t", s_ac, n_lg)
    E.wait("t", s_dv)
    for c in range(8):
        E.op("t", lambda e, c=c: e.transpose(
            ps2[0:128, c * 8:(c + 1) * 8], tl["lg1"][0:8, c * 128:(c + 1) * 128],
            tl["IDF32_sb"][0:8, 0:8]), s_pe)
    n_tr1 = s_pe.n
    E.wait("v", s_pe, n_tr1)
    E.op("v", lambda e: e.tensor_copy(tl["lt1"][:, :], ps2[0:128, 0:64]), s_dv)
    E.wait("v", s_dv)
    E.op("v", lambda e: e.tensor_copy(tl["zs1"][:, :], tl["lt1"][:, :]), s_dv)
    for layer in BATCHER8:
        E.wait("v", s_dv)
        for (i, j) in layer:
            ai = tl["zs1"][:, :].rearrange("p (c e) -> p c e", e=8)[:, :, i]
            aj = tl["zs1"][:, :].rearrange("p (c e) -> p c e", e=8)[:, :, j]
            E.op("v", lambda e, ai=ai, aj=aj: e.tensor_max(
                tl["sp_hi"][:, :], ai, aj), s_dv)
            E.op("v", lambda e, ai=ai, aj=aj: e.tensor_tensor(
                tl["sp_lo"][:, :], ai, aj, ALU.min), s_dv)
            E.wait("v", s_dv)
            E.op("v", lambda e, ai=ai: e.tensor_copy(ai, tl["sp_hi"][:, :]), s_dv)
            E.op("v", lambda e, aj=aj: e.tensor_copy(aj, tl["sp_lo"][:, :]), s_dv)
    n_sort = s_dv.n
    E.wait("v", s_dv, n_sort)
    zsv = tl["zs1"][:, :].rearrange("p (c e) -> p c e", e=8)
    zcv = tl["zc1"][:, :].rearrange("p (c e) -> p c e", e=8)
    E.op("v", lambda e: e.tensor_copy(zcv[:, :, 0], zsv[:, :, 0]), s_dv)
    for k in range(1, 8):
        E.wait("v", s_dv)
        E.op("v", lambda e, k=k: e.tensor_add(zcv[:, :, k], zcv[:, :, k - 1],
                                              zsv[:, :, k]), s_dv)
    n_zc = s_dv.n
    spt = tl["sp_t"][:, :].rearrange("p (c e) -> p c e", e=8)
    spi = tl["sp_i"][:, :].rearrange("p (c e) -> p c e", e=8)
    E.wait("v", s_dv, n_zc)
    for k in range(8):
        E.op("v", lambda e, k=k: e.tensor_scalar(
            spt[:, :, k], zsv[:, :, k], float(k + 1), None, ALU.mult), s_dv)
    E.wait("v", s_dv)
    E.op("v", lambda e: e.tensor_sub(tl["sp_t"][:, :], tl["sp_t"][:, :],
                                     tl["zc1"][:, :]), s_dv)
    E.wait("v", s_dv)
    E.op("v", lambda e: e.tensor_scalar(
        tl["sp_i"][:, :], tl["sp_t"][:, :], -1.0, None, ALU.is_gt), s_dv)
    n_ind = s_dv.n
    E.wait("v", s_dv, n_ind)
    E.op("v", lambda e: e.tensor_reduce(
        tl["kz1"][:, :], tl["sp_i"][:, :].rearrange("p (c e) -> p c e", e=8),
        AX.X, ALU.add), s_dv)
    spd = tl["sp_d"][:, :].rearrange("p (c e) -> p c e", e=8)
    for k in range(7):
        E.op("v", lambda e, k=k: e.tensor_sub(spd[:, :, k], spi[:, :, k],
                                              spi[:, :, k + 1]), s_dv)
    E.op("v", lambda e: e.tensor_copy(spd[:, :, 7], spi[:, :, 7]), s_dv)
    E.wait("v", s_dv)
    E.op("v", lambda e: e.tensor_mul(tl["sp_d"][:, :], tl["sp_d"][:, :],
                                     tl["zc1"][:, :]), s_dv)
    E.wait("v", s_dv)
    E.op("v", lambda e: e.tensor_reduce(
        tl["tau1"][:, :], tl["sp_d"][:, :].rearrange("p (c e) -> p c e", e=8),
        AX.X, ALU.add), s_dv)
    E.wait("v", s_dv)
    E.op("v", lambda e: e.tensor_scalar(
        tl["tau1"][:, :], tl["tau1"][:, :], -1.0, None, ALU.add), s_dv)
    E.wait("v", s_dv)
    E.op("v", lambda e: e.reciprocal(tl["kz1"][:, :], tl["kz1"][:, :]), s_dv)
    E.wait("v", s_dv)
    E.op("v", lambda e: e.tensor_mul(
        tl["tau1"][:, :], tl["tau1"][:, :], tl["kz1"][:, :]), s_dv)
    n_tau = s_dv.n
    E.wait("v", s_dv, n_tau)
    for c in range(8):
        E.op("v", lambda e, c=c: e.tensor_scalar(
            tl["w1t"][:, c * 8:(c + 1) * 8], tl["lt1"][:, c * 8:(c + 1) * 8],
            tl["tau1"][:, c:c + 1], None, ALU.subtract), s_dv)
    E.wait("v", s_dv)
    E.op("v", lambda e: e.tensor_relu(tl["w1t"][:, :], tl["w1t"][:, :]), s_dv)
    n_w1 = s_dv.n
    E.wait("t", s_dv, n_w1)
    for c in range(8):
        E.op("t", lambda e, c=c: e.transpose(
            ps2[0:8, c * 128:(c + 1) * 128], tl["w1t"][0:128, c * 8:(c + 1) * 8],
            tl["IDF32_sb"][:, :]), s_pe)
    n_tr2 = s_pe.n
    E.wait("v", s_pe, n_tr2)
    E.op("v", lambda e: e.tensor_copy(tl["w1T"][:, :], ps2[0:8, 0:1024]), s_dv)
    n_w1T = s_dv.n
    E.wait("t", s_dv, n_w1T)
    for m in range(2):
        E.op("t", lambda e, m=m: e.matmul(
            ps3[0:128, m * 512:(m + 1) * 512], tl["EXPAND16_sb"][0:8, :],
            tl["w1T"][0:8, m * 512:(m + 1) * 512], start=True, stop=True), s_pe)
    n_ex = s_pe.n
    E.wait("v", s_pe, n_ex)
    E.wait("v", s_ac, n_h1)
    E.op("v", lambda e: e.tensor_mul(tl["wh1"][:, :], tl["h1b"][:, :],
                                     ps3[0:128, 0:1024]), s_dv)
    n_wh1 = s_dv.n
    return dict(n_qe=n_qe, n_wh1=n_wh1, n_pe=s_pe.n, n_dv=s_dv.n, n_ac=s_ac.n,
                n_st=s_st.n)


def make_tiles_l1attn(tl, nc, ctx):
    sb = lambda n, s, d: ctx.enter_context(nc.sbuf_tensor(n, s, d))
    tl["Kkv"] = sb("Kkv", [128, 8192], BF16)
    tl["Vkv"] = sb("Vkv", [128, 8192], BF16)
    tl["Ebuf"] = [sb("Ebuf0", [128, 1024], BF16), sb("Ebuf1", [128, 1024], BF16)]
    tl["Rb1"] = sb("Rb1", [128, 1024], F32)
    return tl


def build_l1_attn(E, tl, sm, dr, qinfo):
    s_pe, s_dv, s_ac = sm["pe"], sm["dv"], sm["ac"]
    s_gq, s_sq, s_yq, s_st = sm["gq"], sm["sq"], sm["yq"], sm["st"]
    psp, ps2, ps3 = tl["ps_proj"], tl["ps2"], tl["ps3"]
    KAGv = dr["KAG"].rearrange("(r kv d) t -> kv d r t", r=8, kv=8)
    VAGv = dr["VAG"].rearrange("(r kv cl s) d -> kv s r cl d", r=8, kv=8, cl=8)
    for ek in ("t", "v", "s", "g", "y"):
        E.wait(ek, sm["ac"], qinfo["n_ac"])
        E.wait(ek, sm["dv"], qinfo["n_dv"])
        E.wait(ek, sm["pe"], qinfo["n_pe"])
    n_kfree = 0
    n_vfree = 0
    n_att_prev = 0
    n_rb_prev = 0
    for kv in range(8):
        E.wait("y", s_pe, n_kfree)
        n_kld = E.dma("y", tl["Kkv"][:, :], KAGv[kv], s_yq)
        E.wait("y", s_pe, n_vfree)
        for r in range(8):
            base = (r * 64 + kv * 8) * 128
            n_vld = E.dma(
                "y",
                tl["Vkv"][:, r * 1024:(r + 1) * 1024].rearrange(
                    "p (cl d) -> p cl d", cl=8),
                dr["VAG"][base:base + 1024, :].rearrange("(cl s) d -> s cl d", cl=8),
                s_yq)
        n_exp_prev = 0
        n_ebuf_mm = [0, 0]
        for c in range(64):
            b = c % 2
            E.wait("t", s_yq, n_kld)
            if c == 0:
                E.wait("t", s_dv, n_att_prev)
                E.wait("t", s_dv, n_rb_prev)
                E.wait("t", s_yq, n_vld)
            E.wait("t", s_ac, n_exp_prev)
            for m in range(2):
                E.op("t", lambda e, c=c, m=m, kv=kv: e.matmul(
                    ps2[0:128, m * 512:(m + 1) * 512],
                    tl["Kkv"][:, c * 128:(c + 1) * 128],
                    tl["QE"][:, kv * 1024 + m * 512:kv * 1024 + (m + 1) * 512],
                    start=True, stop=True), s_pe)
            n_smm = s_pe.n
            n_kfree = n_smm
            E.wait("s", s_pe, n_smm)
            E.wait("s", s_pe, n_ebuf_mm[b])
            E.op("s", lambda e, b=b: e.activation(
                tl["Ebuf"][b][:, :], ps2[0:128, 0:1024], AF.Exp,
                bias=float(-EXPC), scale=float(INV_SQRT_HD)), s_ac)
            n_exp = s_ac.n
            n_exp_prev = n_exp
            E.wait("t", s_ac, n_exp)
            for m in range(2):
                E.op("t", lambda e, c=c, m=m, b=b: e.matmul(
                    ps3[0:128, m * 512:(m + 1) * 512],
                    tl["Vkv"][:, c * 128:(c + 1) * 128],
                    tl["Ebuf"][b][:, m * 512:(m + 1) * 512],
                    start=(c == 0), stop=(c == 63)), s_pe)
            E.op("t", lambda e, c=c, b=b: e.matmul(
                psp[0:1, 0:512], tl["ONESCOLB_sb"][:, 0:1],
                tl["Ebuf"][b][:, 0:512], start=(c == 0), stop=(c == 63)), s_pe)
            E.op("t", lambda e, c=c, b=b: e.matmul(
                psp[32:33, 512:1024], tl["ONESCOLB_sb"][:, 0:1],
                tl["Ebuf"][b][:, 512:1024], start=(c == 0), stop=(c == 63)), s_pe)
            n_ebuf_mm[b] = s_pe.n
            n_vfree = s_pe.n
        n_pv = s_pe.n
        E.wait("s", s_pe, n_pv)
        E.op("s", lambda e: e.copy(tl["rrow"][0:1, 0:512], psp[0:1, 0:512]), s_ac)
        E.op("s", lambda e: e.copy(tl["rrow"][0:1, 512:1024], psp[32:33, 512:1024]), s_ac)
        n_rr = s_ac.n
        E.wait("v", s_ac, n_rr)
        E.wait("v", s_dv)
        E.op("v", lambda e: e.reciprocal(tl["rrow"][0:1, 0:1024], tl["rrow"][0:1, 0:1024]), s_dv)
        n_rc = s_dv.n
        E.wait("t", s_dv, n_rc)
        E.wait("t", s_ac, n_exp_prev)
        for m in range(2):
            E.op("t", lambda e, m=m: e.matmul(
                ps2[0:128, m * 512:(m + 1) * 512], tl["ONES1F_sb"][0:1, :],
                tl["rrow"][0:1, m * 512:(m + 1) * 512], start=True, stop=True), s_pe)
        n_rbmm = s_pe.n
        E.wait("v", s_pe, n_rbmm)
        E.op("v", lambda e: e.tensor_copy(tl["Rb1"][:, :], ps2[0:128, 0:1024]), s_dv)
        n_rb_prev = s_dv.n
        E.wait("v", s_dv)
        E.op("v", lambda e, kv=kv: e.tensor_mul(
            tl["ATT"][:, kv * 1024:(kv + 1) * 1024], ps3[0:128, 0:1024],
            tl["Rb1"][:, :]), s_dv)
        n_att_prev = s_dv.n
    return dict(n_att=n_att_prev, n_pe=s_pe.n, n_dv=s_dv.n, n_ac=s_ac.n)


def make_tiles_l1out(tl, nc, ctx):
    sb = lambda n, s, d: ctx.enter_context(nc.sbuf_tensor(n, s, d))
    tl["wo1b"] = [sb("wo1b0", [128, 1024], BF16), sb("wo1b1", [128, 1024], BF16)]
    tl["wb1b"] = [sb("wb1b0", [128, 4096], BF16), sb("wb1b1", [128, 4096], BF16)]
    tl["lb1b"] = sb("lb1b", [128, 4096], BF16)
    tl["bbr"] = sb("bbr", [128, 32], F32)
    tl["obuf2"] = sb("obuf2", [128, 512], BF16)
    return tl


def build_l1_out(E, tl, sm, dr, ainfo):
    s_pe, s_dv, s_ac = sm["pe"], sm["dv"], sm["ac"]
    s_gq, s_sq, s_yq, s_st = sm["gq"], sm["sq"], sm["yq"], sm["st"]
    psp, ps2, ps3 = tl["ps_proj"], tl["ps2"], tl["ps3"]
    XN = tl["XN"][:, :]
    n_bb = E.dma("y", tl["bbr"][:, :], dr["bb_r"], s_yq)
    n_lb = E.dma("g", tl["lb1b"][:, :], dr["lB1_r"], s_gq)
    n_wo_pre = [0, 0]
    n_wb_pre = [0, 0]
    wo_free = [0, 0]
    wb_free = [0, 0]
    n_ps_free = [0, 0]
    n_ob_st = 0
    first = True
    n_wo_pre[0] = E.dma("g", tl["wo1b"][0][:, :], dr["wo_r"][0], s_gq)
    n_wb_pre[0] = E.dma("g", tl["wb1b"][0][:, :], dr["Wb_r"][0], s_gq)
    for half in range(2):
        for dt_ in range(32):
            bb = dt_ % 2
            gidx = half * 32 + dt_
            if gidx + 1 < 64:
                bn = (gidx + 1) % 2
                E.wait("g", s_pe, wo_free[bn])
                n_wo_pre[bn] = E.dma("g", tl["wo1b"][bn][:, :],
                                     dr["wo_r"][(gidx + 1) % 32], s_gq)
                E.wait("g", s_pe, wb_free[bn])
                n_wb_pre[bn] = E.dma("g", tl["wb1b"][bn][:, :],
                                     dr["Wb_r"][(gidx + 1) % 32], s_gq)
            E.wait("t", s_gq, n_wo_pre[bb])
            E.wait("t", s_gq, n_wb_pre[bb])
            if first:
                E.wait("t", s_gq, n_lb)
                first = False
            psl = ps2[:, bb * 512:(bb + 1) * 512]
            if gidx < 2:
                E.wait("t", s_dv, ainfo["n_dv"])
            else:
                E.wait("t", s_ac, n_ps_free[bb])
            for kv in range(8):
                E.op("t", lambda e, kv=kv, bb=bb, half=half, pl=psl: e.matmul(
                    pl, tl["wo1b"][bb][:, kv * 128:(kv + 1) * 128],
                    tl["ATT"][:, kv * 1024 + half * 512:kv * 1024 + (half + 1) * 512],
                    start=(kv == 0), stop=False), s_pe)
            for k in range(32):
                E.op("t", lambda e, k=k, bb=bb, half=half, pl=psl: e.matmul(
                    pl, tl["wb1b"][bb][:, k * 128:(k + 1) * 128],
                    XN[:, k * 1024 + half * 512:k * 1024 + (half + 1) * 512],
                    start=False, stop=False), s_pe)
            E.op("t", lambda e, dt_=dt_, half=half, pl=psl: e.matmul(
                pl, tl["lb1b"][:, dt_ * 128:(dt_ + 1) * 128],
                tl["wh1"][:, half * 512:(half + 1) * 512],
                start=False, stop=True), s_pe)
            n_mm = s_pe.n
            wo_free[bb] = n_mm
            wb_free[bb] = n_mm
            E.wait("s", s_pe, n_mm)
            E.wait("s", s_st, n_ob_st)
            E.wait("s", s_yq, n_bb)
            E.op("s", lambda e, dt_=dt_, pl=psl: e.activation(
                tl["obuf2"][:, :], pl, AF.Identity, bias=tl["bbr"][:, dt_:dt_ + 1]),
                s_ac)
            n_ob = s_ac.n
            n_ps_free[bb] = n_ob
            E.wait("g", s_ac, n_ob)
            n_ob_st = E.dma("g", dr["mf2_out"][dt_ * 128:(dt_ + 1) * 128,
                                               half * 512:(half + 1) * 512],
                            tl["obuf2"][:, :], s_st)
    return dict(n_pe=s_pe.n, n_dv=s_dv.n, n_ac=s_ac.n, n_st=s_st.n)


def build_l1_resid(E, tl, sm, dr, oinfo):
    s_pe, s_dv, s_ac = sm["pe"], sm["dv"], sm["ac"]
    s_gq, s_yq, s_st = sm["gq"], sm["yq"], sm["st"]
    trb = tl["ps_trb"]
    mfv = dr["mf2_out"].rearrange("(dt p) t -> p dt t", dt=32)
    n_mf_free = 0
    n_x1b_free = 0
    n_xot_free = 0
    n_x1f_free = 0
    for tt in range(8):
        E.wait("y", s_pe, n_mf_free)
        n_mf = E.dma("y", tl["mfb"][:, :], mfv[:, :, tt * 128:(tt + 1) * 128], s_yq)
        E.wait("y", s_dv, n_xot_free)
        n_xo = E.dma("y", tl["xot"][:, :], dr["x1"][tt * 128:(tt + 1) * 128, :], s_yq)
        for g4 in range(4):
            E.wait("t", s_yq, n_mf)
            E.wait("t", s_ac)
            E.wait("t", s_dv, n_x1b_free)
            for j in range(8):
                dt_ = g4 * 8 + j
                E.op("t", lambda e, dt_=dt_, j=j: e.transpose(
                    trb[:, j * 128:(j + 1) * 128],
                    tl["mfb"][:, dt_ * 128:(dt_ + 1) * 128],
                    tl["IDBF_sb"][:, :]), s_pe)
            n_tr = s_pe.n
            E.wait("s", s_pe, n_tr)
            E.op("s", lambda e, g4=g4: e.copy(
                tl["x1b"][:, g4 * 1024:(g4 + 1) * 1024], trb[:, 0:1024]), s_ac)
        n_mf_free = s_pe.n
        n_ev = sm["ac"].n
        E.wait("v", s_ac, n_ev)
        E.wait("v", s_yq, n_xo)
        E.wait("v", s_st, n_x1f_free)
        E.op("v", lambda e: e.tensor_add(tl["x1f"][:, :], tl["x1b"][:, :],
                                         tl["xot"][:, :]), s_dv)
        n_a = s_dv.n
        n_xot_free = n_a
        n_x1b_free = n_a
        E.wait("g", s_dv, n_a)
        n_x1f_free = E.dma("g", dr["out"][tt * 128:(tt + 1) * 128, :], tl["x1f"][:, :],
                           s_st)
    return dict(n_st=s_st.n)


from concourse.bass_utils import run_bass_kernel_spmd


def _front(nc, ctx, E, c, x_ap, ntile, slab_w, dst_off, l0_carry):
    tl = make_tiles_l0_persist(nc, ctx, c, l0_carry=l0_carry)
    csc_tile, csc_dram = make_scalar_consts(nc, ctx)
    sm = {k: E.sem(k) for k in ("pe", "dv", "ac", "po", "yq", "gq", "sq", "st", "cc")}
    load_shared_consts(E, tl, sm, c, csc_tile, csc_dram)
    with ExitStack() as p0ctx:
        sb0 = lambda n, s, d: p0ctx.enter_context(nc.sbuf_tensor(n, s, d))
        tl0 = {
            "ln_x": [sb0("ln_x0", [128, 4096], F32), sb0("ln_x1", [128, 4096], F32)],
            "ln_xc": sb0("ln_xc", [128, 4096], F32),
            "ln_sqf": sb0("ln_sqf", [128, 4096], F32),
            "ln_xn": sb0("ln_xn", [128, 4096], BF16),
            "ln_col": sb0("ln_col", [128, 8], F32),
            "ps_trb": tl["ps_trb"], "IDBF_sb": tl["IDBF_sb"],
        }
        build_ln(E, tl0, sm, x_ap, tl["XN"][:, :], ntile, slab_w, dst_off)
    for ek in ("t", "v", "s", "g", "y"):
        E.wait(ek, sm["ac"], sm["ac"].n)
        E.wait(ek, sm["dv"], sm["dv"].n)
        E.wait(ek, sm["pe"], sm["pe"].n)
    return tl, sm


def _gate_all(E, sm, keys=("ac", "dv", "pe")):
    for ek in ("t", "v", "s", "g", "y"):
        for kk in keys:
            E.wait(ek, sm[kk], sm[kk].n)


def build_launch1():
    nc = bass.Bass("TRN2", target_bir_lowering=False, debug=False)
    dr = {}
    x = nc.dram_tensor("x", [TP, D], F32, kind="ExternalInput").ap()
    for nm, sh in (("W_in_r", [131, 128, 4096]), ("convw_r", [128, 264]),
                   ("dtb", [64, 1]), ("alog", [64, 1]), ("dpar", [64, 1])):
        dr[nm] = nc.dram_tensor(nm, sh, F32, kind="ExternalInput").ap()
    dr["convw"] = dr["convw_r"]
    dr["ylocal"] = nc.dram_tensor("ylocal", [8192, 1024], BF16, kind="ExternalOutput").ap()
    dr["S_pack"] = nc.dram_tensor("S_pack", [8192, 128], BF16, kind="ExternalOutput").ap()
    dr["tot_pack"] = nc.dram_tensor("tot_pack", [64, 1], F32, kind="ExternalOutput").ap()
    c = make_consts(nc)
    with ExitStack() as ctx:
        E = Emit(nc, ctx)
        tl, sm = _front(nc, ctx, E, c, x, TP // 128, TP, 0, l0_carry=False)
        tlm = make_tiles_mamba(nc, ctx)
        tl.update(tlm)
        pre = build_mamba_preamble(E, tl, sm, dr)
        build_mamba_phase_a(E, tl, sm, dr, pre, nheads=64)
        E.wait("g", sm["st"])
        E.wait("y", sm["st"])
        block = ctx.enter_context(nc.Block())
        E.install(block)
    return nc


def build_launch2():
    nc = bass.Bass("TRN2", target_bir_lowering=False, debug=False)
    dr = {}
    x = nc.dram_tensor("x", [TP, D], F32, kind="ExternalInput").ap()
    for nm, sh in (("W_in_r", [131, 128, 4096]), ("convw_r", [128, 264]),
                   ("dtb", [64, 1]), ("alog", [64, 1]), ("dpar", [64, 1]),
                   ("rkmask", [8, 1]), ("tot_all", [512, 1]), ("nw_r", [128, 64]),
                   ("A0_r", [128, 4096]), ("routeT", [128, 256]),
                   ("Bw0_r", [128, 4096]), ("W_out_r", [32, 128, 8192])):
        dr[nm] = nc.dram_tensor(nm, sh, F32, kind="ExternalInput").ap()
    dr["S_all"] = nc.dram_tensor("S_all", [65536, 128], BF16, kind="ExternalInput").ap()
    dr["ylocal"] = nc.dram_tensor("ylocal", [8192, 1024], BF16, kind="ExternalInput").ap()
    dr["convw"] = dr["convw_r"]
    dr["x"] = x
    dr["yg"] = nc.dram_tensor("yg", [8192, 1024], BF16, kind="Internal").ap()
    dr["mf_out"] = nc.dram_tensor("mf_out", [4096, 1024], BF16, kind="Internal").ap()
    dr["x1"] = nc.dram_tensor("x1", [1024, 4096], F32, kind="ExternalOutput").ap()
    c = make_consts(nc)
    with ExitStack() as ctx:
        E = Emit(nc, ctx)
        tl, sm = _front(nc, ctx, E, c, x, TP // 128, TP, 0, l0_carry=True)
        with ExitStack() as mctx:
            tlm = make_tiles_mamba(nc, mctx)
            tl.update(tlm)
            pre = build_mamba_preamble(E, tl, sm, dr)
            make_tiles_phase_c(tl, nc, mctx)
            cinfo = build_mamba_phase_c(E, tl, sm, dr, pre, nheads=64)
            winfo = build_ffn0_pre(E, tl, sm, dr, cinfo)
        _gate_all(E, sm)
        with ExitStack() as wctx:
            make_tiles_l0tail(tl, nc, wctx)
            tinfo = build_wout(E, tl, sm, dr, winfo)
        _gate_all(E, sm, keys=("ac", "st", "pe"))
        make_tiles_resid(tl, nc, ctx)
        build_l0_resid(E, tl, sm, dr, tinfo)
        E.wait("g", sm["st"])
        E.wait("y", sm["st"])
        block = ctx.enter_context(nc.Block())
        E.install(block)
    return nc


def build_launch3():
    nc = bass.Bass("TRN2", target_bir_lowering=False, debug=False)
    dr = {}
    x1 = nc.dram_tensor("x1", [1024, D], F32, kind="ExternalInput").ap()
    for nm, sh in (("wq_r", [8, 128, 16384]), ("wk_r", [8, 128, 4096]),
                   ("wv_r", [8, 128, 4096]), ("lA1_r", [128, 4096]),
                   ("wrt_r", [128, 256]), ("brt", [8, 1])):
        dr[nm] = nc.dram_tensor(nm, sh, F32, kind="ExternalInput").ap()
    dr["Kpack"] = nc.dram_tensor("Kpack", [1024, 1024], BF16, kind="ExternalOutput").ap()
    dr["Vpack"] = nc.dram_tensor("Vpack", [8192, 128], BF16, kind="ExternalOutput").ap()
    c = make_consts(nc)
    with ExitStack() as ctx:
        E = Emit(nc, ctx)
        tl, sm = _front(nc, ctx, E, c, x1, 8, 1024, 0, l0_carry=False)
        make_tiles_l1_persist(tl, nc, ctx)
        with ExitStack() as qctx:
            make_tiles_l1qkv(tl, nc, qctx)
            build_l1_qkv(E, tl, sm, dr)
        E.wait("g", sm["st"])
        E.wait("y", sm["st"])
        block = ctx.enter_context(nc.Block())
        E.install(block)
    return nc


def build_launch4():
    nc = bass.Bass("TRN2", target_bir_lowering=False, debug=False)
    dr = {}
    x1 = nc.dram_tensor("x1", [1024, D], F32, kind="ExternalInput").ap()
    for nm, sh in (("wq_r", [8, 128, 16384]), ("wk_r", [8, 128, 4096]),
                   ("wv_r", [8, 128, 4096]), ("lA1_r", [128, 4096]),
                   ("wrt_r", [128, 256]), ("brt", [8, 1]),
                   ("wo_r", [32, 128, 1024]), ("Wb_r", [32, 128, 4096]),
                   ("bb_r", [128, 32]), ("lB1_r", [128, 4096])):
        dr[nm] = nc.dram_tensor(nm, sh, F32, kind="ExternalInput").ap()
    dr["KAG"] = nc.dram_tensor("KAG", [8192, 1024], BF16, kind="ExternalInput").ap()
    dr["VAG"] = nc.dram_tensor("VAG", [65536, 128], BF16, kind="ExternalInput").ap()
    dr["Kpack"] = nc.dram_tensor("Kpack", [1024, 1024], BF16, kind="Internal").ap()
    dr["Vpack"] = nc.dram_tensor("Vpack", [8192, 128], BF16, kind="Internal").ap()
    dr["mf2_out"] = nc.dram_tensor("mf2_out", [4096, 1024], BF16, kind="Internal").ap()
    dr["x1"] = x1
    dr["out"] = nc.dram_tensor("out", [1024, D], F32, kind="ExternalOutput").ap()
    c = make_consts(nc)
    with ExitStack() as ctx:
        E = Emit(nc, ctx)
        tl, sm = _front(nc, ctx, E, c, x1, 8, 1024, 0, l0_carry=False)
        make_tiles_l1_persist(tl, nc, ctx)
        with ExitStack() as qctx:
            make_tiles_l1qkv(tl, nc, qctx)
            qinfo = build_l1_qkv(E, tl, sm, dr)
        with ExitStack() as actx:
            make_tiles_l1attn(tl, nc, actx)
            ainfo = build_l1_attn(E, tl, sm, dr, qinfo)
        _gate_all(E, sm)
        with ExitStack() as octx:
            make_tiles_l1out(tl, nc, octx)
            oinfo = build_l1_out(E, tl, sm, dr, ainfo)
        _gate_all(E, sm, keys=("ac", "st", "pe"))
        make_tiles_resid(tl, nc, ctx)
        build_l1_resid(E, tl, sm, dr, oinfo)
        E.wait("g", sm["st"])
        E.wait("y", sm["st"])
        block = ctx.enter_context(nc.Block())
        E.install(block)
    return nc


def kernel(**inputs):
    h, xs, rms = host_prep(inputs)
    cores = list(range(NCORE))

    nc1 = build_launch1()
    in1 = [{"x": xs[i], "W_in_r": h["W_in_r"], "convw_r": h["convw_r"],
            "dtb": h["dtb"], "alog": h["alog"], "dpar": h["dpar"]}
           for i in range(NCORE)]
    r1 = run_bass_kernel_spmd(nc1, in1, core_ids=cores)
    S_all = np.concatenate([np.asarray(r1.results[i]["S_pack"]) for i in range(NCORE)])
    tot_all = np.concatenate([np.asarray(r1.results[i]["tot_pack"]) for i in range(NCORE)])

    nc2 = build_launch2()
    in2 = [{"x": xs[i], "W_in_r": h["W_in_r"], "convw_r": h["convw_r"],
            "dtb": h["dtb"], "alog": h["alog"], "dpar": h["dpar"],
            "rkmask": rms[i], "tot_all": tot_all, "S_all": S_all,
            "ylocal": np.asarray(r1.results[i]["ylocal"]),
            "nw_r": h["nw_r"], "A0_r": h["A0_r"], "routeT": h["routeT"],
            "Bw0_r": h["Bw0_r"], "W_out_r": h["W_out_r"]}
           for i in range(NCORE)]
    r2 = run_bass_kernel_spmd(nc2, in2, core_ids=cores)
    x1s = [np.asarray(r2.results[i]["x1"]) for i in range(NCORE)]

    nc3 = build_launch3()
    wts3 = {k: h[k] for k in ("wq_r", "wk_r", "wv_r", "lA1_r", "wrt_r", "brt")}
    in3 = [{"x1": x1s[i], **wts3} for i in range(NCORE)]
    r3 = run_bass_kernel_spmd(nc3, in3, core_ids=cores)
    KAG = np.concatenate([np.asarray(r3.results[i]["Kpack"]) for i in range(NCORE)])
    VAG = np.concatenate([np.asarray(r3.results[i]["Vpack"]) for i in range(NCORE)])

    nc4 = build_launch4()
    wts4 = {k: h[k] for k in ("wq_r", "wk_r", "wv_r", "lA1_r", "wrt_r", "brt",
                              "wo_r", "Wb_r", "bb_r", "lB1_r")}
    in4 = [{"x1": x1s[i], "KAG": KAG, "VAG": VAG, **wts4} for i in range(NCORE)]
    r4 = run_bass_kernel_spmd(nc4, in4, core_ids=cores)
    out = np.concatenate([np.asarray(r4.results[i]["out"]) for i in range(NCORE)])
    return out.reshape(1, 8192, 4096)
